# version 02
# baseline (speedup 1.0000x reference)
import numpy as np
import ml_dtypes
from contextlib import ExitStack

import concourse.bass as bass
import concourse.mybir as mybir
from concourse.bass_utils import run_bass_kernel_spmd

F32 = mybir.dt.float32
BF16 = mybir.dt.bfloat16
AF = mybir.ActivationFunctionType
ALU = mybir.AluOpType
AX = mybir.AxisListType.X

NC = 8
T = 2048
D = 1024
NE = 8
NG = 8
NCH = T // 128
EPS_LN = 1e-5
EPS_AT = 1e-6
BF = ml_dtypes.bfloat16

PHASES = ("ffn1", "attn", "ffn2")


class Prog:
    def __init__(self, nc):
        self.nc = nc
        self.ctx = ExitStack()
        sem = lambda n: self.ctx.enter_context(nc.semaphore(n))
        self.sv = sem("sv")
        self.sa = sem("sa")
        self.spe = sem("spe")
        self.sdg = sem("sdg")
        self.sds = sem("sds")
        self.scc = sem("scc")
        self.vc = self.ac = self.pec = 0
        self.dgc = self.dsc = self.ccc = 0
        self.streams = {}
        for nm, kind in (("w0", "g"), ("w1", "g"), ("w2", "g"), ("w3", "g"),
                         ("x0", "s"), ("x1", "s"), ("xg0", "g"), ("xg1", "g"),
                         ("o0", "s"), ("o1", "s")):
            self.streams[nm] = [self.ctx.enter_context(nc.semaphore("st_" + nm)),
                                kind, 0]

    def DQ(self, nm, out, in_):
        sem, kind, _ = self.streams[nm]
        eng = self.nc.gpsimd if kind == "g" else self.nc.sync
        eng.dma_start(out=out, in_=in_).then_inc(sem, 16)
        self.streams[nm][2] += 16
        return self.streams[nm][2]

    def sq(self, nm):
        return self.streams[nm][0]

    def V(self, op, *a, **kw):
        if self.vc > 0:
            self.nc.vector.wait_ge(self.sv, self.vc)
        getattr(self.nc.vector, op)(*a, **kw).then_inc(self.sv, 1)
        self.vc += 1
        return self.vc

    def A(self, op, *a, **kw):
        if self.ac > 0:
            self.nc.scalar.wait_ge(self.sa, self.ac)
        getattr(self.nc.scalar, op)(*a, **kw).then_inc(self.sa, 1)
        self.ac += 1
        return self.ac

    def MM(self, out, lhsT, rhs, start, stop, inc=False, tp=None):
        kw = {"tile_position": tp} if tp is not None else {}
        inst = self.nc.tensor.matmul(out, lhsT, rhs, start=start, stop=stop, **kw)
        if inc:
            inst.then_inc(self.spe, 1)
            self.pec += 1
        return self.pec

    def TR(self, out_ps, in_, ident):
        self.nc.tensor.transpose(out_ps, in_, ident).then_inc(self.spe, 1)
        self.pec += 1
        return self.pec

    def DG(self, out, in_):
        self.nc.gpsimd.dma_start(out=out, in_=in_).then_inc(self.sdg, 16)
        self.dgc += 16
        return self.dgc

    def DS(self, out, in_):
        self.nc.sync.dma_start(out=out, in_=in_).then_inc(self.sds, 16)
        self.dsc += 16
        return self.dsc

    def wT(self, sem, val):
        self.nc.tensor.wait_ge(sem, val)

    def wV(self, sem, val):
        self.nc.vector.wait_ge(sem, val)

    def wA(self, sem, val):
        self.nc.scalar.wait_ge(sem, val)

    def wG(self, sem, val):
        self.nc.gpsimd.wait_ge(sem, val)

    def wS(self, sem, val):
        self.nc.sync.wait_ge(sem, val)

    def drain_barrier(self):
        self.wG(self.sdg, self.dgc)
        self.wS(self.sds, self.dsc)
        for nm, (sem, kind, cntv) in self.streams.items():
            if cntv:
                (self.wG if kind == "g" else self.wS)(sem, cntv)
        self.nc.all_engine_barrier()


def build_program(phases=PHASES, attn_stop=None):
    nc = bass.Bass("TRN2", target_bir_lowering=False)
    dt = nc.dram_tensor

    x_in = dt("x_in", [D, T], F32, kind="ExternalInput").ap()
    wq_p = dt("wq_p", [NG, 128, 8, 128], F32, kind="ExternalInput").ap()
    wk_p = dt("wk_p", [NG, 128, 8, 128], F32, kind="ExternalInput").ap()
    wv_p = dt("wv_p", [NG, 128, 8, 128], F32, kind="ExternalInput").ap()
    wo_p = dt("wo_p", [8, 128, 8, 128], F32, kind="ExternalInput").ap()
    w1_p = dt("w1_p", [NE, 32, 128, 8, 128], F32, kind="ExternalInput").ap()
    w2_p = dt("w2_p", [NE, 8, 128, 32, 128], F32, kind="ExternalInput").ap()
    ws1_p = dt("ws1_p", [32, 128, 8, 128], F32, kind="ExternalInput").ap()
    ws2_p = dt("ws2_p", [8, 128, 32, 128], F32, kind="ExternalInput").ap()
    gw_p = dt("gw_p", [128, 64], F32, kind="ExternalInput").ap()
    gb_p = dt("gb_p", [1, 8], F32, kind="ExternalInput").ap()
    gbb_p = dt("gbb_p", [128, 8], F32, kind="ExternalInput").ap()
    b1_p = dt("b1_p", [NE, 128, 32], F32, kind="ExternalInput").ap()
    b2_p = dt("b2_p", [NE, 128, 8], F32, kind="ExternalInput").ap()
    bs1_p = dt("bs1_p", [128, 32], F32, kind="ExternalInput").ap()
    bs2_p = dt("bs2_p", [128, 8], F32, kind="ExternalInput").ap()
    bq_p = dt("bq_p", [128, 8], F32, kind="ExternalInput").ap()
    bk_p = dt("bk_p", [128, 8], F32, kind="ExternalInput").ap()
    bv_p = dt("bv_p", [128, 8], F32, kind="ExternalInput").ap()
    bo_p = dt("bo_p", [128, 8], F32, kind="ExternalInput").ap()
    umask_p = dt("umask_p", [128, 128], F32, kind="ExternalInput").ap()
    ident_p = dt("ident_p", [128, 128], F32, kind="ExternalInput").ap()
    ones_p = dt("ones_p", [128, 128], F32, kind="ExternalInput").ap()
    sel_p = dt("sel_p", [128, 128], F32, kind="ExternalInput").ap()
    selt_p = dt("selt_p", [16, NG * 128], F32, kind="ExternalInput").ap()
    e8_p = dt("e8_p", [8, NE * 128], F32, kind="ExternalInput").ap()
    n16_p = dt("n16_p", [16, T], F32, kind="ExternalInput").ap()
    rn16_p = dt("rn16_p", [16, T], F32, kind="ExternalInput").ap()
    rmask_p = dt("rmask_p", [128, 8], F32, kind="ExternalInput").ap()

    x_out = dt("x_out", [D, T], F32, kind="ExternalOutput").ap()

    xres = dt("xres", [D, T], F32).ap()
    cc_in = [dt(f"cc{i}_in", [128, 16], F32) for i in range(2)]
    cc_out = [dt(f"cc{i}_out", [128 * NC, 16], F32) for i in range(2)]
    cc3_in = dt("cc3_in", [16, 1], F32)
    cc3_out = dt("cc3_out", [16 * NC, 1], F32)
    cc4_in = dt("cc4_in", [128, NG * 64], F32)
    cc4_out = dt("cc4_out", [128 * NC, NG * 64], F32)

    p = Prog(nc)
    outer = ExitStack()
    sb = lambda n, s, d: outer.enter_context(nc.sbuf_tensor(n, s, d))

    umask = sb("umask", [128, 128], F32)
    ident = sb("ident", [128, 128], BF16)
    ones128 = sb("ones128", [128, 128], BF16)
    onesk1 = sb("onesk1", [1, 128], BF16)
    selc = sb("selc", [128, 128], BF16)
    seltc = sb("seltc", [16, NG * 128], BF16)
    e8sel = sb("e8sel", [8, NE * 128], BF16)
    rmask = sb("rmask", [128, 8], F32)
    b1_sb = sb("b1_sb", [128, NE * 32], F32)
    bs1_sb = sb("bs1_sb", [128, 32], F32)
    b2_sb = sb("b2_sb", [128, NE * 8], F32)
    bs2h_sb = sb("bs2h_sb", [128, 8], F32)
    bqkv_sb = sb("bqkv_sb", [128, 24], F32)
    bo_sb = sb("bo_sb", [128, 8], F32)
    gw_sb = sb("gw_sb", [128, 64], BF16)
    gb_sb = sb("gb_sb", [1, 8], BF16)
    gwf_sb = sb("gwf_sb", [128, 64], F32)
    gwlo_sb = sb("gwlo_sb", [128, 64], BF16)
    gbb_sb = sb("gbb_sb", [128, 8], F32)
    ec5 = sb("ec5", [128, 1], F32)
    halfc = sb("halfc", [128, 1], F32)

    def load_consts():
        p.DS(umask[:, :], umask_p)
        p.DG(ident[:, :], ident_p)
        p.DG(ones128[:, :], ones_p)
        p.DG(onesk1[:, :], ones_p[0:1, :])
        p.DG(selc[:, :], sel_p)
        p.DG(seltc[:, :], selt_p)
        p.DG(e8sel[:, :], e8_p)
        p.DS(rmask[:, :], rmask_p)
        for e in range(NE):
            p.DS(b1_sb[:, e * 32:(e + 1) * 32], b1_p[e])
            p.DS(b2_sb[:, e * 8:(e + 1) * 8], b2_p[e])
        p.DS(bs1_sb[:, :], bs1_p)
        p.DS(bqkv_sb[:, 0:8], bq_p)
        p.DS(bqkv_sb[:, 8:16], bk_p)
        p.DS(bqkv_sb[:, 16:24], bv_p)
        p.DS(bo_sb[:, :], bo_p)
        p.DG(gw_sb[:, :], gw_p)
        p.DG(gb_sb[:, :], gb_p)
        p.DS(gwf_sb[:, :], gw_p)
        p.DS(gbb_sb[:, :], gbb_p)
        p.DS(bs2h_sb[:, :], bs2_p)
        p.wV(p.sds, p.dsc)
        p.wV(p.sdg, p.dgc)
        p.V("memset", ec5[:, :], EPS_LN)
        p.V("memset", halfc[:, :], 0.5)
        p.V("tensor_scalar_mul", bs2h_sb[:, :], bs2h_sb[:, :], 0.5)
        p.V("tensor_sub", gwlo_sb[:, :], gwf_sb[:, :], gw_sb[:, :])

    def layernorm(src, c0, xn_slice, xc2, cb, sqb, t1, mbc, rstd, ps_sx, ps_sq):
        last_mm = 0
        cast_done = [0, 0]
        for ch in range(8):
            t = xc2[ch % 2]
            if ch >= 2:
                p.wS(p.sa, cast_done[ch % 2])
            dv = p.DQ(f"x{ch % 2}", t[:, :], src[ch * 128:(ch + 1) * 128, c0:c0 + 1024])
            p.wA(p.sq(f"x{ch % 2}"), dv)
            if ch >= 1:
                p.wA(p.spe, last_mm)
            p.A("activation", cb[:, :], t[:, :], AF.Copy)
            cast_done[ch % 2] = p.A("activation", sqb[:, :], t[:, :], AF.Square)
            p.wT(p.sa, cast_done[ch % 2])
            for n in range(2):
                sl = slice(n * 512, (n + 1) * 512)
                p.MM(ps_sx[:, sl], ones128[:, :], cb[:, sl], start=(ch == 0),
                     stop=(ch == 7))
                last_mm = p.MM(ps_sq[:, sl], ones128[:, :], sqb[:, sl],
                               start=(ch == 0), stop=(ch == 7), inc=True)
        p.wV(p.spe, last_mm)
        p.V("tensor_scalar_mul", mbc[:, :], ps_sx, 1.0 / D)
        p.V("tensor_mul", rstd[:, :], mbc[:, :], mbc[:, :])
        p.V("scalar_tensor_tensor", rstd[:, :], ps_sq, 1.0 / D, rstd[:, :],
            ALU.mult, ALU.subtract)
        p.wA(p.sv, p.vc)
        sq_v = p.A("activation", rstd[:, :], rstd[:, :], AF.Sqrt, bias=ec5[:, :])
        p.wV(p.sa, sq_v)
        p.V("reciprocal", rstd[:, :], rstd[:, :])
        p.wS(p.sa, p.ac)
        for ch in range(8):
            t = xc2[ch % 2]
            dv = p.DQ(f"x{ch % 2}", t[:, :], src[ch * 128:(ch + 1) * 128, c0:c0 + 1024])
            p.wV(p.sq(f"x{ch % 2}"), dv)
            p.V("tensor_sub", t1[:, :], t[:, :], mbc[:, :])
            f32dst, bf16dst = xn_slice(ch)
            if f32dst is not None:
                p.V("tensor_mul", f32dst, t1[:, :], rstd[:, :])
                p.V("tensor_copy", bf16dst, f32dst)
                p.wS(p.sv, p.vc - 2)
            else:
                p.V("tensor_mul", bf16dst, t1[:, :], rstd[:, :])
                p.wS(p.sv, p.vc - 1)
        return p.vc

    def ffn_call(x_src, x_dst, tag):
        st = ExitStack()
        fsb = lambda n, s, d: st.enter_context(nc.sbuf_tensor(f"{tag}{n}", s, d))
        fps = lambda n, s, d: st.enter_context(nc.psum_tensor(f"{tag}{n}", s, d))
        xn = fsb("xn", [128, 8 * 1024], BF16)
        xnf = fsb("xnf", [128, 8 * 1024], F32)
        h = fsb("h", [128, 32 * 1024], BF16)
        y = fsb("y", [128, 8 * 1024], F32)
        w1b = [fsb(f"w1b{i}", [128, 8 * 128], BF16) for i in range(2)]
        w2b = [fsb(f"w2b{i}", [128, 32 * 128], BF16) for i in range(2)]
        xc2 = [fsb(f"xc{i}", [128, 1024], F32) for i in range(2)]
        cbt = fsb("cbt", [128, 1024], BF16)
        sqbt = fsb("sqbt", [128, 1024], BF16)
        t1 = fsb("t1", [128, 1024], F32)
        mbc = fsb("mbc", [128, 1024], F32)
        rstd = fsb("rstd", [128, 1024], F32)
        combT = fsb("combT", [8, 8 * 128], BF16)
        cbc = xnf[:, 3072:4096]
        ztmp = xnf[:, 2048:3072]
        xo2 = [xnf[:, 0:1024], xnf[:, 1024:2048]]
        g8 = [fsb(f"g8_{i}", [128, 8], F32) for i in range(5)]
        gcl = [fsb(f"gc_{i}", [128, 1], F32) for i in range(3)]
        gtb = fsb("gtb", [128, 8], BF16)
        psA = fps("psA", [128, 2048], F32)
        psB = fps("psB", [128, 1024], F32)
        psC = fps("psC", [128, 512], F32)
        psT = fps("psT", [128, 1024], BF16)

        mm1_grp = [0] * 32
        mm1_evac = [0] * 32
        mm2_grp = [0] * 8
        mm2_yacc = [0] * 8
        xout_dma = [0] * 8
        prev_ztmp_read = 0

        for half in range(2):
            c0 = half * 1024
            if half == 1:
                p.wT(p.sa, p.ac)
                p.wT(p.sv, p.vc)
                p.wS(p.sv, p.vc)
                p.wV(p.sa, p.ac)
            lnv = layernorm(x_src, c0,
                            lambda ch: (xnf[:, ch * 1024:(ch + 1) * 1024],
                                        xn[:, ch * 1024:(ch + 1) * 1024]),
                            xc2, cbt, sqbt, t1, mbc, rstd,
                            psA[:, 0:1024], psA[:, 1024:2048])
            xnlo = h[:, 0:8 * 1024]
            for ch in range(8):
                p.V("tensor_sub", xnlo[:, ch * 1024:(ch + 1) * 1024],
                    xnf[:, ch * 1024:(ch + 1) * 1024], xn[:, ch * 1024:(ch + 1) * 1024])
            p.wT(p.sv, p.vc)

            lg_evac = 0
            for c in range(8):
                if c > 0:
                    p.wT(p.sv, lg_evac)
                for k in range(8):
                    ksl = slice(k * 1024 + c * 128, k * 1024 + (c + 1) * 128)
                    gsl = slice(k * 8, (k + 1) * 8)
                    p.MM(psC[:, 0:8], xn[:, ksl], gw_sb[:, gsl], start=(k == 0), stop=False)
                    p.MM(psC[:, 0:8], xnlo[:, ksl], gw_sb[:, gsl], start=False, stop=False)
                    gmm = p.MM(psC[:, 0:8], xn[:, ksl], gwlo_sb[:, gsl], start=False,
                               stop=(k == 7), inc=(k == 7))
                p.wV(p.spe, gmm)
                lg = g8[0]
                p.V("tensor_copy", lg[:, :], psC[:, 0:8])
                lg_evac = p.V("tensor_add", lg[:, :], lg[:, :], gbb_sb[:, :])
                p.V("reduce_max", gcl[0][:, :], lg[:, :], AX)
                p.V("tensor_single_scalar", g8[1][:, :], lg[:, :], gcl[0][:, :], ALU.is_ge)
                p.V("scalar_tensor_tensor", g8[2][:, :], g8[1][:, :], -1e30, lg[:, :],
                    ALU.mult, ALU.add)
                p.V("reduce_max", gcl[1][:, :], g8[2][:, :], AX)
                dv_ = p.V("tensor_single_scalar", g8[3][:, :], lg[:, :], gcl[0][:, :],
                          ALU.subtract)
                p.wA(p.sv, dv_)
                ev = p.A("activation", g8[3][:, :], g8[3][:, :], AF.Exp)
                m21 = p.V("tensor_single_scalar", gcl[2][:, :], gcl[1][:, :], gcl[0][:, :],
                          ALU.subtract)
                p.wA(p.sv, m21)
                e2v = p.A("activation", gcl[2][:, :], gcl[2][:, :], AF.Exp)
                p.wV(p.sa, e2v)
                p.V("tensor_scalar_add", gcl[2][:, :], gcl[2][:, :], 1.0)
                p.V("reciprocal", gcl[2][:, :], gcl[2][:, :])
                p.V("tensor_single_scalar", g8[4][:, :], lg[:, :], gcl[1][:, :], ALU.is_ge)
                p.wV(p.sa, ev)
                p.V("tensor_mul", g8[4][:, :], g8[4][:, :], g8[3][:, :])
                p.V("tensor_single_scalar", g8[4][:, :], g8[4][:, :], gcl[2][:, :], ALU.mult)
                p.V("tensor_scalar_mul", g8[4][:, :], g8[4][:, :], 0.5)
                cv = p.V("tensor_copy", gtb[:, :], g8[4][:, :])
                p.wT(p.sv, cv)
                trv = p.TR(psT[0:8, 0:128], gtb[:, :], ident[:, :])
                p.wV(p.spe, trv)
                p.V("tensor_copy", combT[0:8, c * 128:(c + 1) * 128], psT[0:8, 0:128])
            gate_v = p.vc

            for e in range(NE + 1):
                shared = (e == NE)
                for dc in range(32):
                    buf = w1b[dc % 2]
                    first = (half == 0 and e == 0 and dc < 2)
                    if not first:
                        p.wG(p.spe, mm1_grp[dc - 2] if dc >= 2 else mm1_grp[30 + dc])
                    src = ws1_p[dc] if shared else w1_p[e, dc]
                    dv = p.DQ(f"w{dc % 2}", buf[:, :], src.rearrange("r k c -> r (k c)"))
                    p.wT(p.sq(f"w{dc % 2}"), dv)
                    pr = psA[:, (dc % 2) * 1024:(dc % 2 + 1) * 1024]
                    if not first:
                        p.wT(p.sa, mm1_evac[dc - 2] if dc >= 2 else mm1_evac[30 + dc])
                    if dc == 0:
                        p.wT(p.sv, p.vc)
                    for k in range(8):
                        p.MM(pr[:, 0:512], buf[:, k * 128:(k + 1) * 128],
                             xn[:, k * 1024:k * 1024 + 512], start=(k == 0), stop=(k == 7))
                        p.MM(pr[:, 512:1024], buf[:, k * 128:(k + 1) * 128],
                             xn[:, k * 1024 + 512:(k + 1) * 1024], start=(k == 0),
                             stop=(k == 7), inc=(k == 7))
                    mm1_grp[dc] = p.pec
                    bias = bs1_sb[:, dc:dc + 1] if shared else \
                        b1_sb[:, e * 32 + dc:e * 32 + dc + 1]
                    p.wA(p.spe, mm1_grp[dc])
                    mm1_evac[dc] = p.A("activation", h[:, dc * 1024:(dc + 1) * 1024], pr,
                                       AF.Gelu, bias=bias)
                comb_v = 0
                if not shared:
                    p.wT(p.sv, gate_v)
                    p.wT(p.sv, mm2_yacc[6])
                    for c in range(8):
                        p.MM(psB[:, c * 128:(c + 1) * 128],
                             e8sel[:, e * 128:(e + 1) * 128],
                             combT[0:8, c * 128:(c + 1) * 128],
                             start=True, stop=True, inc=(c == 7))
                    p.wV(p.spe, p.pec)
                    comb_v = p.V("tensor_copy", cbc, psB[:, 0:1024])
                for dm in range(8):
                    buf = w2b[dm % 2]
                    first2 = (half == 0 and e == 0 and dm < 2)
                    if not first2:
                        p.wG(p.spe, mm2_grp[dm - 2] if dm >= 2 else mm2_grp[6 + dm])
                    src = ws2_p[dm] if shared else w2_p[e, dm]
                    dv = p.DQ(f"w{2 + dm % 2}", buf[:, :], src.rearrange("r k c -> r (k c)"))
                    p.wT(p.sq(f"w{2 + dm % 2}"), dv)
                    pr = psB[:, 0:1024] if dm % 2 == 0 else psA[:, 0:1024]
                    if dm % 2 == 0 and not shared:
                        p.wT(p.sv, comb_v)
                    if dm == 0 or dm % 2 == 1:
                        p.wT(p.sa, mm1_evac[31])
                    if not first2:
                        p.wT(p.sv, mm2_yacc[dm - 2] if dm >= 2 else mm2_yacc[6 + dm])
                    for k in range(32):
                        p.MM(pr[:, 0:512], buf[:, k * 128:(k + 1) * 128],
                             h[:, k * 1024:k * 1024 + 512], start=(k == 0), stop=(k == 31))
                        p.MM(pr[:, 512:1024], buf[:, k * 128:(k + 1) * 128],
                             h[:, k * 1024 + 512:(k + 1) * 1024], start=(k == 0),
                             stop=(k == 31), inc=(k == 31))
                    mm2_grp[dm] = p.pec
                    p.wA(p.spe, mm2_grp[dm])
                    p.wA(p.sv, prev_ztmp_read)
                    if shared:
                        zv = p.A("activation", ztmp, pr, AF.Identity,
                                 bias=bs2h_sb[:, dm:dm + 1], scale=halfc[:, :])
                        p.wV(p.sa, zv)
                        prev_ztmp_read = mm2_yacc[dm] = p.V(
                            "tensor_add", y[:, dm * 1024:(dm + 1) * 1024],
                            y[:, dm * 1024:(dm + 1) * 1024], ztmp)
                    else:
                        zv = p.A("activation", ztmp, pr, AF.Identity,
                                 bias=b2_sb[:, e * 8 + dm:e * 8 + dm + 1])
                        p.wV(p.sa, zv)
                        if e == 0:
                            prev_ztmp_read = mm2_yacc[dm] = p.V(
                                "tensor_mul", y[:, dm * 1024:(dm + 1) * 1024],
                                ztmp, cbc)
                        else:
                            p.V("tensor_mul", ztmp, ztmp, cbc)
                            prev_ztmp_read = mm2_yacc[dm] = p.V(
                                "tensor_add", y[:, dm * 1024:(dm + 1) * 1024],
                                y[:, dm * 1024:(dm + 1) * 1024], ztmp)
            uvh = [0] * 8
            for ch in range(8):
                t = xc2[ch % 2]
                o = xo2[ch % 2]
                if ch >= 2:
                    p.wS(p.sq(f"o{ch % 2}"), xout_dma[ch - 2])
                    p.wS(p.sv, uvh[ch - 2])
                dv = p.DQ(f"x{ch % 2}", t[:, :], x_src[ch * 128:(ch + 1) * 128, c0:c0 + 1024])
                p.wV(p.sq(f"x{ch % 2}"), dv)
                if ch >= 2:
                    p.wV(p.sq(f"o{ch % 2}"), xout_dma[ch - 2])
                uvh[ch] = uv = p.V("tensor_add", o, t[:, :], y[:, ch * 1024:(ch + 1) * 1024])
                p.wS(p.sv, uv)
                xout_dma[ch] = p.DQ(f"o{ch % 2}", x_dst[ch * 128:(ch + 1) * 128, c0:c0 + 1024], o)
            for par in range(2):
                p.wV(p.sq(f"o{par}"), p.streams[f"o{par}"][2])
        st.close()

    dbg = dt("dbg", [128, 52224], F32, kind="ExternalOutput").ap() if attn_stop else None

    def attention(x_src, x_dst):
        st = ExitStack()
        asb = lambda n, s, d: st.enter_context(nc.sbuf_tensor("at" + n, s, d))
        outu = asb("outu", [128, NG * T], BF16)
        q_fm = asb("qfm", [128, NG * T], BF16)
        k_fm = asb("kfm", [128, NG * T], BF16)
        v_fm = asb("vfm", [128, NG * T], BF16)
        tp1 = asb("tp1", [16, T], F32)
        u1 = asb("u1", [128, T], F32)
        tp3 = u1[0:16, :]
        rs1b = asb("rs1b", [16, T], BF16)
        sab = asb("sab", [16, T], BF16)
        scb = asb("scb", [16, T], BF16)
        gb_ = asb("gb_", [16, T], BF16)

        s1 = ExitStack()
        s1sb = lambda n, s, d: s1.enter_context(nc.sbuf_tensor("a1" + n, s, d))
        s1ps = lambda n, s, d: s1.enter_context(nc.psum_tensor("a1" + n, s, d))
        n16 = s1sb("n16", [16, T], F32)
        rn16 = s1sb("rn16", [16, T], F32)
        ckscr = s1sb("cks", [128, T], F32)
        wprod = s1sb("wpr", [128, T], BF16)
        sib = scb
        sob = gb_
        wvb_t = s1sb("wvb", [128, 8 * 128], BF16)
        wvb = [wvb_t, wvb_t]
        gath = s1sb("gath", [128, 16], F32)
        pref1 = s1sb("pf1", [128, 16], F32)
        pref2 = s1sb("pf2", [128, 16], F32)
        pub = s1sb("pub", [128, 16], F32)
        te_t = s1sb("te", [16, 1], F32)
        ga16 = s1sb("ga16", [16, 1], F32)
        epref = s1sb("epf", [16, 1], F32)
        xc2 = [s1sb(f"xc{i}", [128, 1024], F32) for i in range(2)]
        cbt = wprod[:, 0:1024]
        sqbt = wprod[:, 1024:2048]
        t1 = ckscr[:, 0:1024]
        mbc = u1[:, 0:1024]
        rstd = u1[:, 1024:2048]
        atA = s1ps("atA", [128, 2048], F32)
        atS = s1ps("atS", [16, 2048], F32)

        p.DS(n16[:, :], n16_p)
        p.DS(rn16[:, :], rn16_p)
        p.wV(p.sds, p.dsc)

        def chk(name, dumps):
            if attn_stop != name:
                return False
            p.wG(p.sv, p.vc)
            p.wG(p.sa, p.ac)
            p.wG(p.spe, p.pec)
            col = 0
            for ap, w in dumps:
                rows = ap.shape[0]
                nc.gpsimd.dma_start(out=dbg[0:rows, col:col + w], in_=ap,
                                    allow_slow_non_contiguous=True).then_inc(p.sdg, 16)
                p.dgc += 16
                col += w
            p.wG(p.sdg, p.dgc)
            return True

        for half in range(2):
            c0 = half * 1024
            if half == 1:
                p.wT(p.sv, p.vc)
                p.wS(p.sv, p.vc)
            layernorm(x_src, c0,
                      lambda ch: (None, outu[:, ch * T + c0:ch * T + c0 + 1024]),
                      xc2, cbt, sqbt, t1, mbc, rstd,
                      atA[:, 0:1024], atA[:, 1024:2048])
        p.wT(p.sv, p.vc)

        proj_grp = [0, 0]
        pair_evac = [0, 0]
        for pi, (wp, dest, func, bofs) in enumerate(
                ((wq_p, q_fm, AF.Sigmoid, 0), (wk_p, k_fm, AF.Sigmoid, 8),
                 (wv_p, v_fm, AF.Identity, 16))):
            for g in range(NG):
                idx = pi * NG + g
                buf = wvb[idx % 2]
                if idx >= 1:
                    p.wG(p.spe, max(proj_grp))
                dv = p.DQ("w0", buf[:, :], wp[g].rearrange("r k c -> r (k c)"))
                p.wT(p.sq("w0"), dv)
                for t in range(4):
                    pr = atA[:, (t % 2) * 512:(t % 2) * 512 + 512]
                    p.wT(p.sa, pair_evac[t % 2])
                    for k in range(8):
                        p.MM(pr, buf[:, k * 128:(k + 1) * 128],
                             outu[:, k * T + t * 512:k * T + (t + 1) * 512],
                             start=(k == 0), stop=(k == 7), inc=(k == 7))
                    proj_grp[idx % 2] = p.pec
                    p.wA(p.spe, p.pec)
                    pair_evac[t % 2] = p.A(
                        "activation", dest[:, g * T + t * 512:g * T + (t + 1) * 512],
                        pr, func, bias=bqkv_sb[:, bofs + g:bofs + g + 1])
        p.wV(p.sa, p.ac)
        if chk("proj", [(q_fm[:, :], NG * T), (k_fm[:, :], NG * T),
                        (v_fm[:, :], NG * T)]):
            s1.close(); st.close(); return

        for g in range(NG):
            p.V("tensor_reduce", pub[:, g:g + 1], k_fm[:, g * T:(g + 1) * T], AX, ALU.add)
            p.V("tensor_reduce", pub[:, 8 + g:8 + g + 1], q_fm[:, g * T:(g + 1) * T], AX,
                ALU.add)

        def exchange128(pub_ap, cci, cco, pref_t):
            p.wG(p.sv, p.vc)
            dv = p.DG(cci.ap(), pub_ap)
            p.wG(p.sdg, dv)
            nc.gpsimd.collective_compute(
                "AllGather", ALU.bypass, replica_groups=[list(range(NC))],
                ins=[cci.ap().opt()], outs=[cco.ap().opt()]).then_inc(p.scc, 1)
            p.ccc += 1
            p.wG(p.scc, p.ccc)
            p.V("memset", pref_t[:, :], 0.0)
            for r in range(NC):
                p.wG(p.sv, p.vc)
                dv = p.DG(gath[:, :], cco.ap()[r * 128:(r + 1) * 128, :])
                p.wV(p.sdg, dv)
                p.V("scalar_tensor_tensor", pref_t[:, :], gath[:, :],
                    rmask[:, r:r + 1], pref_t[:, :], ALU.mult, ALU.add)
            return p.vc

        exchange128(pub[:, :], cc_in[0], cc_out[0], pref1)

        def scan_w_sel(data_fm, other_fm, pref_col, g, first, last):
            p.V("tensor_tensor_scan", ckscr[:, :], data_fm,
                data_fm[:, 0:1].broadcast_to((128, T)), pref_col, ALU.add, ALU.bypass)
            p.V("tensor_scalar_add", ckscr[:, :], ckscr[:, :], EPS_AT)
            wv = p.V("scalar_tensor_tensor", wprod[:, :], other_fm, EPS_AT,
                     ckscr[:, :], ALU.add, ALU.mult)
            p.wT(p.sv, wv)
            for t in range(4):
                p.MM(atS[0:16, t * 512:(t + 1) * 512], selc[:, g * 16:(g + 1) * 16],
                     wprod[:, t * 512:(t + 1) * 512], start=first, stop=last,
                     inc=(t == 3))
            return p.pec

        for g in range(NG):
            if g > 0:
                p.wV(p.spe, p.pec)
            scan_w_sel(k_fm[:, g * T:(g + 1) * T], q_fm[:, g * T:(g + 1) * T],
                       pref1[:, g:g + 1], g, g == 0, g == 7)
        p.wV(p.spe, p.pec)
        p.V("reciprocal", tp1[:, :], atS[0:16, :])
        p.V("tensor_copy", rs1b[:, :], tp1[:, :])
        sib_v = p.V("tensor_mul", sib[:, :], n16[:, :], tp1[:, :])
        p.wT(p.sv, p.vc)
        for g in range(NG):
            if g > 0:
                p.wV(p.spe, p.pec)
            scan_w_sel(q_fm[:, g * T:(g + 1) * T], k_fm[:, g * T:(g + 1) * T],
                       pref1[:, 8 + g:8 + g + 1], g, g == 0, g == 7)
        p.wV(p.spe, p.pec)
        p.V("reciprocal", tp1[:, :], atS[0:16, :])
        sob_v = p.V("tensor_mul", sob[:, :], n16[:, :], tp1[:, :])
        if chk("s12", [(sib[0:16, :], T), (sob[0:16, :], T), (pref1[:, :], 16),
                       (pub[:, :], 16)]):
            s1.close(); st.close(); return

        def bc16(src_b16, g, ps):
            for t in range(4):
                p.MM(ps[:, t * 512:(t + 1) * 512], seltc[:, g * 128:(g + 1) * 128],
                     src_b16[:, t * 512:(t + 1) * 512], start=True, stop=True,
                     inc=(t == 3))
            return p.pec

        p.wT(p.sv, sob_v)
        for g in range(NG):
            mmv = bc16(sob, g, atA)
            p.wV(p.spe, mmv)
            kso_v = p.V("scalar_tensor_tensor", wprod[:, :], k_fm[:, g * T:(g + 1) * T],
                        1.0, atA[:, :], ALU.mult, ALU.mult, accum_out=pub[:, g:g + 1])
            p.wT(p.sv, kso_v)
            mmv = bc16(sib, g, atA)
            p.wV(p.spe, mmv)
            qsi_v = p.V("scalar_tensor_tensor", wprod[:, :], q_fm[:, g * T:(g + 1) * T],
                        1.0, atA[:, :], ALU.mult, ALU.mult, accum_out=pub[:, 8 + g:8 + g + 1])
            p.wT(p.sv, qsi_v)

        exchange128(pub[:, :], cc_in[1], cc_out[1], pref2)

        def c34_pass(bcast_b, base_fm_getter, dot_fm_getter, pref_ofs):
            for g in range(NG):
                if g > 0:
                    p.wV(p.spe, p.pec)
                mmv = bc16(bcast_b, g, atA)
                p.wV(p.spe, mmv)
                p.V("scalar_tensor_tensor", wprod[:, :], base_fm_getter(g), 1.0,
                    atA[:, :], ALU.mult, ALU.mult)
                p.V("tensor_tensor_scan", ckscr[:, :], wprod[:, :],
                    wprod[:, 0:1].broadcast_to((128, T)),
                    pref2[:, pref_ofs + g:pref_ofs + g + 1], ALU.add, ALU.bypass)
                p.V("tensor_scalar_add", ckscr[:, :], ckscr[:, :], EPS_AT)
                wv = p.V("scalar_tensor_tensor", wprod[:, :], dot_fm_getter(g), EPS_AT,
                         ckscr[:, :], ALU.add, ALU.mult)
                p.wT(p.sv, wv)
                for t in range(4):
                    p.MM(atS[0:16, t * 512:(t + 1) * 512], selc[:, g * 16:(g + 1) * 16],
                         wprod[:, t * 512:(t + 1) * 512], start=(g == 0), stop=(g == 7),
                         inc=(t == 3))
            return p.pec

        kg = lambda g: k_fm[:, g * T:(g + 1) * T]
        qg = lambda g: q_fm[:, g * T:(g + 1) * T]
        c34_pass(sob, kg, qg, 0)
        p.wV(p.spe, p.pec)
        csk = p.V("tensor_mul", tp1[:, :], atS[0:16, :], rn16[:, :])
        p.wA(p.sv, csk)
        sa_v = p.A("activation", sab[:, :], tp1[:, :], AF.Sigmoid)
        p.wT(p.sv, csk)
        c34_pass(sib, qg, kg, 8)
        p.wV(p.spe, p.pec)
        p.wV(p.sa, sa_v)
        css = p.V("tensor_mul", tp1[:, :], atS[0:16, :], rn16[:, :])
        p.wA(p.sv, css)
        e_v = p.A("activation", tp1[:, :], tp1[:, :], AF.Exp)
        p.wV(p.sa, e_v)
        if chk("c34", [(sab[0:16, :], T), (tp1[0:16, :], T), (pref2[:, :], 16)]):
            s1.close(); st.close(); return

        tev = p.V("tensor_reduce", te_t[:, :], tp1[:, :], AX, ALU.add)
        p.wG(p.sv, tev)
        dv = p.DG(cc3_in.ap(), te_t[:, :])
        p.wG(p.sdg, dv)
        nc.gpsimd.collective_compute(
            "AllGather", ALU.bypass, replica_groups=[list(range(NC))],
            ins=[cc3_in.ap().opt()], outs=[cc3_out.ap().opt()]).then_inc(p.scc, 1)
        p.ccc += 1
        p.wG(p.scc, p.ccc)
        p.V("memset", epref[:, :], 0.0)
        for r in range(NC):
            p.wG(p.sv, p.vc)
            dv = p.DG(ga16[:, :], cc3_out.ap()[r * 16:(r + 1) * 16, :])
            p.wV(p.sdg, dv)
            p.V("scalar_tensor_tensor", epref[:, :], ga16[:, :], rmask[0:16, r:r + 1],
                epref[:, :], ALU.mult, ALU.add)

        p.V("tensor_tensor_scan", tp3[:, :], tp1[:, :],
            tp1[:, 0:1].broadcast_to((16, T)), epref[:, :], ALU.add, ALU.bypass)
        p.V("reciprocal", tp3[:, :], tp3[:, :])
        p.V("tensor_mul", tp3[:, :], tp1[:, :], tp3[:, :])
        p.V("tensor_mul", scb[:, :], tp3[:, :], n16[:, :])
        gv = p.V("tensor_mul", gb_[:, :], sab[:, :], rs1b[:, :])
        p.wT(p.sv, gv)
        if chk("sc", [(scb[0:16, :], T), (gb_[0:16, :], T), (epref[0:16, :], 1)]):
            s1.close(); st.close(); return
        s1.close()

        s2 = ExitStack()
        s2sb = lambda n, s, d: s2.enter_context(nc.sbuf_tensor("a2" + n, s, d))
        s2ps = lambda n, s, d: s2.enter_context(nc.psum_tensor("a2" + n, s, d))
        gbc = s2sb("gbc", [128, T], BF16)
        pub4 = s2sb("pb4", [128, NG * 64], F32)
        sgth = s2sb("sgt", [128, NG * 64], F32)
        spref_b = s2sb("spb", [128, NG * 64], BF16)
        ktm = [s2sb(f"ktm{i}", [128, 128], BF16) for i in range(2)]
        vtm = [s2sb(f"vtm{i}", [128, 128], BF16) for i in range(2)]
        amu = [s2sb(f"amu{i}", [128, 256], BF16) for i in range(2)]
        snap = s2sb("snp", [128, 128], BF16)
        s_acc = s2sb("sacc", [128, 128], F32)
        e1t = [s2sb(f"e1t{i}", [128, 256], F32) for i in range(2)]
        wosb = [s2sb(f"wo{i}", [128, 8 * 128], BF16) for i in range(2)]
        zt5 = s2sb("zt5", [128, 512], F32)
        xc5 = [s2sb(f"xc5{i}", [128, 512], F32) for i in range(2)]
        xo5 = [s2sb(f"xo5{i}", [128, 512], F32) for i in range(2)]
        pBig = s2ps("pB", [128, 2048], F32)
        pS = s2ps("pS", [128, 128], F32)
        pT = s2ps("pT", [128, 2048], BF16)
        pW = s2ps("pW", [128, 512], F32)

        tr_evac = [0, 0]
        amu_evac = [0, 0]
        out_evac = [0, 0]
        supd = 0
        j_ng, j_nc, j_steps = NG, NCH, 5
        if attn_stop and attn_stop.startswith("j:"):
            parts = attn_stop.split(":")
            j_ng, j_nc = int(parts[1]), int(parts[2])
            if len(parts) > 3:
                j_steps = int(parts[3])
        for g in range(j_ng):
            gofs = g * T
            p.wT(p.sv, p.vc)
            mmv = bc16(scb, g, pBig)
            p.wV(p.spe, mmv)
            vp = p.V("scalar_tensor_tensor", v_fm[:, gofs:gofs + T],
                     v_fm[:, gofs:gofs + T], 1.0, pBig[:, :], ALU.mult, ALU.mult)
            p.wT(p.sv, vp)
            mmv = bc16(gb_, g, pBig)
            p.wV(p.spe, mmv)
            gvv = p.V("tensor_copy", gbc[:, :], pBig[:, :])
            p.V("memset", s_acc[:, :], 0.0)
            p.V("memset", snap[:, :], 0.0)
            p.wT(p.sv, p.vc)
            for c in range(j_nc):
                if j_nc == 0:
                    break
                pa = c % 2
                cs = slice(gofs + c * 128, gofs + (c + 1) * 128)
                tslot = pT[:, pa * 1024:pa * 1024 + 256]
                p.wT(p.sv, tr_evac[pa])
                p.TR(tslot[:, 0:128], k_fm[:, cs], ident[:, :])
                trv = p.TR(tslot[:, 128:256], v_fm[:, cs], ident[:, :])
                p.wV(p.spe, trv)
                p.V("tensor_copy", ktm[pa][:, :], tslot[:, 0:128])
                tr_evac[pa] = p.V("tensor_copy", vtm[pa][:, :], tslot[:, 128:256])
                if j_steps < 2:
                    continue
                pgA = pBig[:, 0:128]
                pgB = pBig[:, 512:640]
                p.wT(p.sv, amu_evac[0])
                p.MM(pgA, k_fm[0:64, cs], q_fm[0:64, cs], start=True, stop=True,
                     tp=(0, 0))
                gmv = p.MM(pgB, k_fm[64:128, cs], q_fm[64:128, cs],
                           start=True, stop=True, tp=(64, 0), inc=True)
                p.wV(p.spe, gmv)
                p.V("tensor_mul", amu[pa][:, 0:128], pgA, umask[:, :])
                amu_evac[0] = p.V("tensor_mul", amu[pa][:, 128:256], pgB,
                                  umask[:, :])
                if j_steps < 3:
                    continue
                poi = pBig[:, 1024:1152]
                pon = pBig[:, 1536:1664]
                p.wT(p.sv, out_evac[0])
                p.wT(p.sv, amu_evac[0])
                p.wT(p.sv, tr_evac[pa])
                p.MM(poi[0:64, :], vtm[pa][:, 0:64], amu[pa][:, 0:128],
                     start=True, stop=True, tp=(0, 0))
                p.MM(poi[64:128, :], vtm[pa][:, 64:128], amu[pa][:, 128:256],
                     start=True, stop=True, tp=(0, 64))
                if c > 0:
                    p.wT(p.sv, sdone)
                    p.MM(pon[0:64, :], snap[0:64, 0:64], q_fm[0:64, cs],
                         start=True, stop=True, tp=(0, 0))
                    p.MM(pon[64:128, :], snap[64:128, 64:128], q_fm[64:128, cs],
                         start=True, stop=True, tp=(64, 64))
                if j_steps < 4:
                    continue
                if c > 0:
                    p.wT(p.sv, sdone)
                supd = p.MM(pS[:, :], ktm[pa][:, :], vtm[pa][:, :], start=True,
                            stop=True, inc=True)
                p.wV(p.spe, supd)
                p.V("tensor_add", s_acc[:, :], s_acc[:, :], pS[:, :])
                sdone = p.V("tensor_copy", snap[:, :], s_acc[:, :])
                if j_steps < 5:
                    continue
                if c > 0:
                    p.V("tensor_mul", e1t[pa][:, 0:128], poi,
                        gbc[:, c * 128:(c + 1) * 128])
                    p.V("tensor_mul", e1t[pa][:, 128:256], pon,
                        gbc[:, c * 128:(c + 1) * 128])
                    out_evac[0] = p.V("tensor_add", outu[:, cs], e1t[pa][:, 0:128],
                                      e1t[pa][:, 128:256])
                else:
                    out_evac[0] = p.V("tensor_mul", outu[:, cs], poi,
                                      gbc[:, 0:128])
            p.V("tensor_copy", pub4[0:64, g * 64:(g + 1) * 64], s_acc[0:64, 0:64])
            sv_ = p.V("tensor_copy", pub4[64:128, g * 64:(g + 1) * 64],
                      s_acc[64:128, 64:128])
            p.wT(p.sv, sv_)

        if chk("j", [(outu[:, :], NG * T), (pub4[:, :], NG * 64)]) or \
           (attn_stop and attn_stop.startswith("j:") and
            chk(attn_stop, [(outu[:, :], NG * T), (pub4[:, :], NG * 64)])):
            s2.close(); st.close(); return
        p.wG(p.sv, p.vc)
        dv = p.DG(cc4_in.ap(), pub4[:, :])
        p.wG(p.sdg, dv)
        nc.gpsimd.collective_compute(
            "AllGather", ALU.bypass, replica_groups=[list(range(NC))],
            ins=[cc4_in.ap().opt()], outs=[cc4_out.ap().opt()]).then_inc(p.scc, 1)
        p.ccc += 1
        p.wG(p.scc, p.ccc)
        p.wV(p.scc, p.ccc)
        p.V("memset", pub4[:, :], 0.0)
        for r in range(NC):
            p.wG(p.sv, p.vc)
            dv = p.DG(sgth[:, :], cc4_out.ap()[r * 128:(r + 1) * 128, :])
            p.wV(p.sdg, dv)
            p.V("scalar_tensor_tensor", pub4[:, :], sgth[:, :], rmask[:, r:r + 1],
                pub4[:, :], ALU.mult, ALU.add)
        spv = p.V("tensor_copy", spref_b[:, :], pub4[:, :])

        for g in range(NG):
            gofs = g * T
            p.wT(p.sv, p.vc)
            mmv = bc16(gb_, g, pBig)
            p.wV(p.spe, mmv)
            gvv = p.V("tensor_copy", gbc[:, :], pBig[:, :])
            p.wT(p.sv, gvv)
            p.wT(p.sv, spv)
            for t in range(4):
                pr = pW[:, :] if t % 2 == 0 else pBig[:, 0:512]
                ts = slice(gofs + t * 512, gofs + (t + 1) * 512)
                p.MM(pr[0:64, :], spref_b[0:64, g * 64:(g + 1) * 64],
                     q_fm[0:64, ts], start=True, stop=True, tp=(0, 0))
                lmv = p.MM(pr[64:128, :], spref_b[64:128, g * 64:(g + 1) * 64],
                           q_fm[64:128, ts], start=True, stop=True, tp=(64, 64),
                           inc=True)
                p.wV(p.spe, lmv)
                p.V("tensor_mul", zt5[:, :], pr, gbc[:, t * 512:(t + 1) * 512])
                ov = p.V("tensor_add", outu[:, ts], outu[:, ts], zt5[:, :])
                p.wT(p.sv, ov)

        if chk("l", [(outu[:, :], NG * T)]):
            s2.close(); st.close(); return
        if attn_stop == "xres":
            p.wG(p.sv, p.vc)
            for ch in range(8):
                p.DG(dbg[:, ch * 512:(ch + 1) * 512], x_src[ch * 128:(ch + 1) * 128, 0:512])
            p.wG(p.sdg, p.dgc)
            s2.close(); st.close(); return
        wo_grp = [0, 0]
        xo_dma = [0] * 32
        uv_hist = [0] * 32
        prev_z_read = p.vc
        p.wA(p.sv, p.vc)
        p.wT(p.sv, p.vc)
        for dm in range(8):
            buf = wosb[dm % 2]
            if dm >= 2:
                p.wG(p.spe, wo_grp[dm % 2])
            dv = p.DQ(f"w{dm % 2}", buf[:, :], wo_p[dm].rearrange("r k c -> r (k c)"))
            p.wT(p.sq(f"w{dm % 2}"), dv)
            for t in range(4):
                n = dm * 4 + t
                pr = pW[:, :] if t % 2 == 0 else pBig[:, 0:512]
                if n >= 2:
                    p.wT(p.sa, z_evac[n - 2])
                for g in range(NG):
                    p.MM(pr, buf[:, g * 128:(g + 1) * 128],
                         outu[:, g * T + t * 512:g * T + (t + 1) * 512],
                         start=(g == 0), stop=(g == 7), inc=(g == 7))
                wo_grp[dm % 2] = p.pec
                p.wA(p.spe, p.pec)
                p.wA(p.sv, prev_z_read)
                if n == 0:
                    z_evac = [0] * 32
                z_evac[n] = p.A("activation", zt5[:, :], pr, AF.Identity,
                                bias=bo_sb[:, dm:dm + 1])
                if attn_stop == "wo" and dm == 0 and t in (0, 1):
                    p.wG(p.sa, z_evac[n])
                    nc.gpsimd.dma_start(out=dbg[:, t * 512:(t + 1) * 512],
                                        in_=zt5[:, :]).then_inc(p.sdg, 16)
                    p.dgc += 16
                xt_ = xc5[n % 2]
                xo_ = xo5[n % 2]
                if n >= 2:
                    p.wG(p.sq(f"o{n % 2}"), xo_dma[n - 2])
                    p.wG(p.sv, uv_hist[n - 2])
                dvx = p.DQ(f"xg{n % 2}", xt_[:, :], x_src[dm * 128:(dm + 1) * 128,
                                                          t * 512:(t + 1) * 512])
                p.wV(p.sq(f"xg{n % 2}"), dvx)
                p.wV(p.sa, z_evac[n])
                if n >= 2:
                    p.wV(p.sq(f"o{n % 2}"), xo_dma[n - 2])
                prev_z_read = uv = p.V("tensor_add", xo_[:, :], xt_[:, :], zt5[:, :])
                uv_hist[n] = uv
                p.wS(p.sv, uv)
                xo_dma[n] = p.DQ(f"o{n % 2}",
                                 x_dst[dm * 128:(dm + 1) * 128, t * 512:(t + 1) * 512],
                                 xo_[:, :])
                if attn_stop == "n0" and n == 0:
                    p.wG(p.sv, uv)
                    p.DG(dbg[:, 0:512], xt_[:, :])
                    p.DG(dbg[:, 512:1024], zt5[:, :])
                    p.DG(dbg[:, 1024:1536], xo_[:, :])
                    p.wG(p.sdg, p.dgc)
        for par in range(2):
            p.wV(p.sq(f"o{par}"), p.streams[f"o{par}"][2])
        if attn_stop == "xres2":
            p.wG(p.sds, p.dsc)
            for ch in range(8):
                p.DG(dbg[:, ch * 512:(ch + 1) * 512], x_dst[ch * 128:(ch + 1) * 128, 0:512])
            p.wG(p.sdg, p.dgc)
        s2.close()
        st.close()

    load_consts()
    p.drain_barrier()
    if "ffn1" in phases:
        ffn_call(x_in, xres, "f1")
        p.drain_barrier()
    else:
        for ch in range(8):
            p.DS(xres[ch * 128:(ch + 1) * 128, :], x_in[ch * 128:(ch + 1) * 128, :])
        p.drain_barrier()
    if "attn" in phases:
        attention(xres, xres)
        p.drain_barrier()
    if "ffn2" in phases:
        ffn_call(xres, x_out, "f2")
    else:
        for ch in range(8):
            p.DS(x_out[ch * 128:(ch + 1) * 128, :], xres[ch * 128:(ch + 1) * 128, :])
    p.wS(p.sds, p.dsc)
    p.wG(p.sdg, p.dgc)
    for nm, (sem_, kind, cntv) in p.streams.items():
        if cntv:
            (p.wG if kind == "g" else p.wS)(sem_, cntv)
    p.ctx.close()
    outer.close()
    return nc



def _shared_inputs(inputs):
    f32c = lambda a: np.ascontiguousarray(np.asarray(a, dtype=np.float32))
    wq = f32c(inputs["L0_attn_Wq"]); wk = f32c(inputs["L0_attn_Wk"])
    wv = f32c(inputs["L0_attn_Wv"]); wo = f32c(inputs["L0_attn_Wo"])
    w1 = f32c(inputs["L0_ffn_w1"]); w2 = f32c(inputs["L0_ffn_w2"])
    ws1 = f32c(inputs["L0_ffn_ws1"]); ws2 = f32c(inputs["L0_ffn_ws2"])

    def blk(w, kc, dcc):
        return np.ascontiguousarray(
            w.reshape(kc, 128, dcc, 128).transpose(2, 1, 0, 3))

    d = {}
    d["wq_p"] = blk(wq, 8, 8)
    d["wk_p"] = blk(wk, 8, 8)
    d["wv_p"] = blk(wv, 8, 8)
    d["wo_p"] = blk(wo, 8, 8)
    d["w1_p"] = np.ascontiguousarray(
        w1.reshape(NE, 8, 128, 32, 128).transpose(0, 3, 2, 1, 4))
    d["w2_p"] = np.ascontiguousarray(
        w2.reshape(NE, 32, 128, 8, 128).transpose(0, 3, 2, 1, 4))
    d["ws1_p"] = blk(ws1, 8, 32)
    d["ws2_p"] = blk(ws2, 32, 8)
    gw = f32c(inputs["L0_ffn_gate_w"])
    d["gw_p"] = np.ascontiguousarray(
        gw.reshape(8, 128, 8).transpose(1, 0, 2).reshape(128, 64))
    d["gb_p"] = f32c(inputs["L0_ffn_gate_b"]).reshape(1, 8)
    d["gbb_p"] = np.ascontiguousarray(
        np.broadcast_to(f32c(inputs["L0_ffn_gate_b"]).reshape(1, 8), (128, 8)))
    d["b1_p"] = np.ascontiguousarray(
        f32c(inputs["L0_ffn_b1"]).reshape(NE, 32, 128).transpose(0, 2, 1))
    d["b2_p"] = np.ascontiguousarray(
        f32c(inputs["L0_ffn_b2"]).reshape(NE, 8, 128).transpose(0, 2, 1))
    d["bs1_p"] = np.ascontiguousarray(f32c(inputs["L0_ffn_bs1"]).reshape(32, 128).T)
    d["bs2_p"] = np.ascontiguousarray(f32c(inputs["L0_ffn_bs2"]).reshape(8, 128).T)
    d["bq_p"] = np.ascontiguousarray(f32c(inputs["L0_attn_bq"]).reshape(8, 128).T)
    d["bk_p"] = np.ascontiguousarray(f32c(inputs["L0_attn_bk"]).reshape(8, 128).T)
    d["bv_p"] = np.ascontiguousarray(f32c(inputs["L0_attn_bv"]).reshape(8, 128).T)
    d["bo_p"] = np.ascontiguousarray(f32c(inputs["L0_attn_bo"]).reshape(8, 128).T)
    d["umask_p"] = np.triu(np.ones((128, 128), np.float32))
    d["ident_p"] = np.eye(128, dtype=np.float32)
    d["ones_p"] = np.ones((128, 128), np.float32)
    sel = np.zeros((128, 128), np.float32)
    for g in range(NG):
        sel[0:64, g * 16 + 2 * g] = 1.0
        sel[64:128, g * 16 + 2 * g + 1] = 1.0
    d["sel_p"] = sel
    selt = np.zeros((16, NG * 128), np.float32)
    for g in range(NG):
        selt[2 * g, g * 128:g * 128 + 64] = 1.0
        selt[2 * g + 1, g * 128 + 64:g * 128 + 128] = 1.0
    d["selt_p"] = selt
    e8 = np.zeros((8, NE * 128), np.float32)
    for e in range(NE):
        e8[e, e * 128:(e + 1) * 128] = 1.0
    d["e8_p"] = e8
    return d


def make_in_maps(inputs, nc_build=None):
    shared = _shared_inputs(inputs)
    x = np.asarray(inputs["L0_attn_x"], dtype=np.float32).reshape(NC * T, D)
    in_maps = []
    for i in range(NC):
        m = dict(shared)
        m["x_in"] = np.ascontiguousarray(x[i * T:(i + 1) * T, :].T)
        pos = (np.arange(T, dtype=np.float64) + 1.0 + i * T)
        m["n16_p"] = np.ascontiguousarray(
            np.broadcast_to(pos.astype(np.float32), (16, T)))
        m["rn16_p"] = np.ascontiguousarray(
            np.broadcast_to((1.0 / pos).astype(np.float32), (16, T)))
        rm = np.zeros((128, 8), np.float32)
        rm[:, :i] = 1.0
        m["rmask_p"] = rm
        in_maps.append(m)
    return in_maps


def kernel(**inputs):
    nc = build_program()
    in_maps = make_in_maps(inputs)
    res = run_bass_kernel_spmd(nc, in_maps, core_ids=list(range(NC)))
    out = np.concatenate([np.asarray(r["x_out"]).T for r in res.results], axis=0)
    return out.reshape(1, NC * T, D)


# version 03
# speedup vs baseline: 1.0034x; 1.0034x over previous
import numpy as np
import ml_dtypes
from contextlib import ExitStack

import concourse.bass as bass
import concourse.mybir as mybir
from concourse.bass_utils import run_bass_kernel_spmd

F32 = mybir.dt.float32
BF16 = mybir.dt.bfloat16
AF = mybir.ActivationFunctionType
ALU = mybir.AluOpType
AX = mybir.AxisListType.X

NC = 8
T = 2048
D = 1024
NE = 8
NG = 8
NCH = T // 128
EPS_LN = 1e-5
EPS_AT = 1e-6
BF = ml_dtypes.bfloat16

PHASES = ("ffn1", "attn", "ffn2")


class Prog:
    def __init__(self, nc):
        self.nc = nc
        self.ctx = ExitStack()
        sem = lambda n: self.ctx.enter_context(nc.semaphore(n))
        self.sv = sem("sv")
        self.sa = sem("sa")
        self.spe = sem("spe")
        self.sdg = sem("sdg")
        self.sds = sem("sds")
        self.scc = sem("scc")
        self.vc = self.ac = self.pec = 0
        self.dgc = self.dsc = self.ccc = 0
        self.streams = {}
        for nm, kind in (("w0", "g"), ("w1", "g"), ("w2", "g"), ("w3", "g"),
                         ("x0", "s"), ("x1", "s"), ("xg0", "g"), ("xg1", "g"),
                         ("o0", "s"), ("o1", "s")):
            self.streams[nm] = [self.ctx.enter_context(nc.semaphore("st_" + nm)),
                                kind, 0]

    def DQ(self, nm, out, in_):
        sem, kind, _ = self.streams[nm]
        eng = self.nc.gpsimd if kind == "g" else self.nc.sync
        eng.dma_start(out=out, in_=in_).then_inc(sem, 16)
        self.streams[nm][2] += 16
        return self.streams[nm][2]

    def sq(self, nm):
        return self.streams[nm][0]

    def V(self, op, *a, **kw):
        if self.vc > 0:
            self.nc.vector.wait_ge(self.sv, self.vc)
        getattr(self.nc.vector, op)(*a, **kw).then_inc(self.sv, 1)
        self.vc += 1
        return self.vc

    def A(self, op, *a, **kw):
        if self.ac > 0:
            self.nc.scalar.wait_ge(self.sa, self.ac)
        getattr(self.nc.scalar, op)(*a, **kw).then_inc(self.sa, 1)
        self.ac += 1
        return self.ac

    def MM(self, out, lhsT, rhs, start, stop, inc=False, tp=None):
        kw = {"tile_position": tp} if tp is not None else {}
        inst = self.nc.tensor.matmul(out, lhsT, rhs, start=start, stop=stop, **kw)
        if inc:
            inst.then_inc(self.spe, 1)
            self.pec += 1
        return self.pec

    def TR(self, out_ps, in_, ident):
        self.nc.tensor.transpose(out_ps, in_, ident).then_inc(self.spe, 1)
        self.pec += 1
        return self.pec

    def DG(self, out, in_):
        self.nc.gpsimd.dma_start(out=out, in_=in_).then_inc(self.sdg, 16)
        self.dgc += 16
        return self.dgc

    def DS(self, out, in_):
        self.nc.sync.dma_start(out=out, in_=in_).then_inc(self.sds, 16)
        self.dsc += 16
        return self.dsc

    def wT(self, sem, val):
        self.nc.tensor.wait_ge(sem, val)

    def wV(self, sem, val):
        self.nc.vector.wait_ge(sem, val)

    def wA(self, sem, val):
        self.nc.scalar.wait_ge(sem, val)

    def wG(self, sem, val):
        self.nc.gpsimd.wait_ge(sem, val)

    def wS(self, sem, val):
        self.nc.sync.wait_ge(sem, val)

    def drain_barrier(self):
        self.wG(self.sdg, self.dgc)
        self.wS(self.sds, self.dsc)
        for nm, (sem, kind, cntv) in self.streams.items():
            if cntv:
                (self.wG if kind == "g" else self.wS)(sem, cntv)
        self.nc.all_engine_barrier()


def build_program(phases=PHASES, attn_stop=None):
    nc = bass.Bass("TRN2", target_bir_lowering=False)
    dt = nc.dram_tensor

    x_in = dt("x_in", [D, T], F32, kind="ExternalInput").ap()
    wq_p = dt("wq_p", [NG, 128, 8, 128], F32, kind="ExternalInput").ap()
    wk_p = dt("wk_p", [NG, 128, 8, 128], F32, kind="ExternalInput").ap()
    wv_p = dt("wv_p", [NG, 128, 8, 128], F32, kind="ExternalInput").ap()
    wo_p = dt("wo_p", [8, 128, 8, 128], F32, kind="ExternalInput").ap()
    w1_p = dt("w1_p", [NE, 32, 128, 8, 128], F32, kind="ExternalInput").ap()
    w2_p = dt("w2_p", [NE, 8, 128, 32, 128], F32, kind="ExternalInput").ap()
    ws1_p = dt("ws1_p", [32, 128, 8, 128], F32, kind="ExternalInput").ap()
    ws2_p = dt("ws2_p", [8, 128, 32, 128], F32, kind="ExternalInput").ap()
    gw_p = dt("gw_p", [128, 64], F32, kind="ExternalInput").ap()
    gb_p = dt("gb_p", [1, 8], F32, kind="ExternalInput").ap()
    gbb_p = dt("gbb_p", [128, 8], F32, kind="ExternalInput").ap()
    b1_p = dt("b1_p", [NE, 128, 32], F32, kind="ExternalInput").ap()
    b2_p = dt("b2_p", [NE, 128, 8], F32, kind="ExternalInput").ap()
    bs1_p = dt("bs1_p", [128, 32], F32, kind="ExternalInput").ap()
    bs2_p = dt("bs2_p", [128, 8], F32, kind="ExternalInput").ap()
    bq_p = dt("bq_p", [128, 8], F32, kind="ExternalInput").ap()
    bk_p = dt("bk_p", [128, 8], F32, kind="ExternalInput").ap()
    bv_p = dt("bv_p", [128, 8], F32, kind="ExternalInput").ap()
    bo_p = dt("bo_p", [128, 8], F32, kind="ExternalInput").ap()
    umask_p = dt("umask_p", [128, 128], F32, kind="ExternalInput").ap()
    ident_p = dt("ident_p", [128, 128], F32, kind="ExternalInput").ap()
    ones_p = dt("ones_p", [128, 128], F32, kind="ExternalInput").ap()
    sel_p = dt("sel_p", [128, 128], F32, kind="ExternalInput").ap()
    seleps_p = dt("seleps_p", [128, 128], F32, kind="ExternalInput").ap()
    selt_p = dt("selt_p", [16, NG * 128], F32, kind="ExternalInput").ap()
    e8_p = dt("e8_p", [8, NE * 128], F32, kind="ExternalInput").ap()
    n16_p = dt("n16_p", [16, T], F32, kind="ExternalInput").ap()
    rn16_p = dt("rn16_p", [16, T], F32, kind="ExternalInput").ap()
    rmask_p = dt("rmask_p", [128, 8], F32, kind="ExternalInput").ap()

    x_out = dt("x_out", [D, T], F32, kind="ExternalOutput").ap()

    xres = dt("xres", [D, T], F32).ap()
    cc_in = [dt(f"cc{i}_in", [128, 16], F32) for i in range(2)]
    cc_out = [dt(f"cc{i}_out", [128 * NC, 16], F32) for i in range(2)]
    cc3_in = dt("cc3_in", [16, 1], F32)
    cc3_out = dt("cc3_out", [16 * NC, 1], F32)
    cc4_in = dt("cc4_in", [128, NG * 64], F32)
    cc4_out = dt("cc4_out", [128 * NC, NG * 64], F32)

    p = Prog(nc)
    outer = ExitStack()
    sb = lambda n, s, d: outer.enter_context(nc.sbuf_tensor(n, s, d))

    umask = sb("umask", [128, 128], F32)
    ident = sb("ident", [128, 128], BF16)
    ones128 = sb("ones128", [128, 128], BF16)
    onesk1 = sb("onesk1", [1, 128], BF16)
    selc = sb("selc", [128, 128], BF16)
    seleps = sb("seleps", [128, 128], BF16)
    seltc = sb("seltc", [16, NG * 128], BF16)
    e8sel = sb("e8sel", [8, NE * 128], BF16)
    rmask = sb("rmask", [128, 8], F32)
    b1_sb = sb("b1_sb", [128, NE * 32], F32)
    bs1_sb = sb("bs1_sb", [128, 32], F32)
    b2_sb = sb("b2_sb", [128, NE * 8], F32)
    bs2h_sb = sb("bs2h_sb", [128, 8], F32)
    bqkv_sb = sb("bqkv_sb", [128, 24], F32)
    bo_sb = sb("bo_sb", [128, 8], F32)
    gw_sb = sb("gw_sb", [128, 64], BF16)
    gb_sb = sb("gb_sb", [1, 8], BF16)
    gwf_sb = sb("gwf_sb", [128, 64], F32)
    gwlo_sb = sb("gwlo_sb", [128, 64], BF16)
    gbb_sb = sb("gbb_sb", [128, 8], F32)
    ec5 = sb("ec5", [128, 1], F32)
    halfc = sb("halfc", [128, 1], F32)

    def load_consts():
        p.DS(umask[:, :], umask_p)
        p.DG(ident[:, :], ident_p)
        p.DG(ones128[:, :], ones_p)
        p.DG(onesk1[:, :], ones_p[0:1, :])
        p.DG(selc[:, :], sel_p)
        p.DG(seleps[:, :], seleps_p)
        p.DG(seltc[:, :], selt_p)
        p.DG(e8sel[:, :], e8_p)
        p.DS(rmask[:, :], rmask_p)
        for e in range(NE):
            p.DS(b1_sb[:, e * 32:(e + 1) * 32], b1_p[e])
            p.DS(b2_sb[:, e * 8:(e + 1) * 8], b2_p[e])
        p.DS(bs1_sb[:, :], bs1_p)
        p.DS(bqkv_sb[:, 0:8], bq_p)
        p.DS(bqkv_sb[:, 8:16], bk_p)
        p.DS(bqkv_sb[:, 16:24], bv_p)
        p.DS(bo_sb[:, :], bo_p)
        p.DG(gw_sb[:, :], gw_p)
        p.DG(gb_sb[:, :], gb_p)
        p.DS(gwf_sb[:, :], gw_p)
        p.DS(gbb_sb[:, :], gbb_p)
        p.DS(bs2h_sb[:, :], bs2_p)
        p.wV(p.sds, p.dsc)
        p.wV(p.sdg, p.dgc)
        p.V("memset", ec5[:, :], EPS_LN)
        p.V("memset", halfc[:, :], 0.5)
        p.V("tensor_scalar_mul", bs2h_sb[:, :], bs2h_sb[:, :], 0.5)
        p.V("tensor_sub", gwlo_sb[:, :], gwf_sb[:, :], gw_sb[:, :])

    def layernorm(src, c0, xn_slice, xc2, cb, sqb, t1, mbc, rstd, ps_sx, ps_sq):
        last_mm = 0
        cast_done = [0, 0]
        for ch in range(8):
            t = xc2[ch % 2]
            if ch >= 2:
                p.wS(p.sa, cast_done[ch % 2])
            dv = p.DQ(f"x{ch % 2}", t[:, :], src[ch * 128:(ch + 1) * 128, c0:c0 + 1024])
            p.wA(p.sq(f"x{ch % 2}"), dv)
            if ch >= 1:
                p.wA(p.spe, last_mm)
            p.A("activation", cb[:, :], t[:, :], AF.Copy)
            cast_done[ch % 2] = p.A("activation", sqb[:, :], t[:, :], AF.Square)
            p.wT(p.sa, cast_done[ch % 2])
            for n in range(2):
                sl = slice(n * 512, (n + 1) * 512)
                p.MM(ps_sx[:, sl], ones128[:, :], cb[:, sl], start=(ch == 0),
                     stop=(ch == 7))
                last_mm = p.MM(ps_sq[:, sl], ones128[:, :], sqb[:, sl],
                               start=(ch == 0), stop=(ch == 7), inc=True)
        p.wV(p.spe, last_mm)
        p.V("tensor_scalar_mul", mbc[:, :], ps_sx, 1.0 / D)
        p.V("tensor_mul", rstd[:, :], mbc[:, :], mbc[:, :])
        p.V("scalar_tensor_tensor", rstd[:, :], ps_sq, 1.0 / D, rstd[:, :],
            ALU.mult, ALU.subtract)
        p.wA(p.sv, p.vc)
        sq_v = p.A("activation", rstd[:, :], rstd[:, :], AF.Sqrt, bias=ec5[:, :])
        p.wV(p.sa, sq_v)
        p.V("reciprocal", rstd[:, :], rstd[:, :])
        p.wS(p.sa, p.ac)
        for ch in range(8):
            t = xc2[ch % 2]
            dv = p.DQ(f"x{ch % 2}", t[:, :], src[ch * 128:(ch + 1) * 128, c0:c0 + 1024])
            p.wV(p.sq(f"x{ch % 2}"), dv)
            p.V("tensor_sub", t1[:, :], t[:, :], mbc[:, :])
            f32dst, bf16dst = xn_slice(ch)
            if f32dst is not None:
                p.V("tensor_mul", f32dst, t1[:, :], rstd[:, :])
                p.V("tensor_copy", bf16dst, f32dst)
                p.wS(p.sv, p.vc - 2)
            else:
                p.V("tensor_mul", bf16dst, t1[:, :], rstd[:, :])
                p.wS(p.sv, p.vc - 1)
        return p.vc

    def ffn_call(x_src, x_dst, tag):
        st = ExitStack()
        fsb = lambda n, s, d: st.enter_context(nc.sbuf_tensor(f"{tag}{n}", s, d))
        fps = lambda n, s, d: st.enter_context(nc.psum_tensor(f"{tag}{n}", s, d))
        xn = fsb("xn", [128, 8 * 1024], BF16)
        xnf = fsb("xnf", [128, 8 * 1024], F32)
        h = fsb("h", [128, 32 * 1024], BF16)
        y = fsb("y", [128, 8 * 1024], F32)
        w1b = [fsb(f"w1b{i}", [128, 8 * 128], BF16) for i in range(2)]
        w2b = [fsb(f"w2b{i}", [128, 32 * 128], BF16) for i in range(2)]
        xc2 = [fsb(f"xc{i}", [128, 1024], F32) for i in range(2)]
        cbt = fsb("cbt", [128, 1024], BF16)
        sqbt = fsb("sqbt", [128, 1024], BF16)
        t1 = fsb("t1", [128, 1024], F32)
        mbc = fsb("mbc", [128, 1024], F32)
        rstd = fsb("rstd", [128, 1024], F32)
        combT = fsb("combT", [8, 8 * 128], BF16)
        cbc = xnf[:, 3072:4096]
        ztmp = xnf[:, 2048:3072]
        xo2 = [xnf[:, 0:1024], xnf[:, 1024:2048]]
        g8 = [fsb(f"g8_{i}", [128, 8], F32) for i in range(5)]
        gcl = [fsb(f"gc_{i}", [128, 1], F32) for i in range(3)]
        gtb = fsb("gtb", [128, 8], BF16)
        psA = fps("psA", [128, 2048], F32)
        psB = fps("psB", [128, 1024], F32)
        psC = fps("psC", [128, 512], F32)
        psT = fps("psT", [128, 1024], BF16)

        mm1_grp = [0] * 32
        mm1_evac = [0] * 32
        mm2_grp = [0] * 8
        mm2_yacc = [0] * 8
        xout_dma = [0] * 8
        prev_ztmp_read = 0

        for half in range(2):
            c0 = half * 1024
            if half == 1:
                p.wT(p.sa, p.ac)
                p.wT(p.sv, p.vc)
                p.wS(p.sv, p.vc)
                p.wV(p.sa, p.ac)
            lnv = layernorm(x_src, c0,
                            lambda ch: (xnf[:, ch * 1024:(ch + 1) * 1024],
                                        xn[:, ch * 1024:(ch + 1) * 1024]),
                            xc2, cbt, sqbt, t1, mbc, rstd,
                            psA[:, 0:1024], psA[:, 1024:2048])
            xnlo = h[:, 0:8 * 1024]
            for ch in range(8):
                p.V("tensor_sub", xnlo[:, ch * 1024:(ch + 1) * 1024],
                    xnf[:, ch * 1024:(ch + 1) * 1024], xn[:, ch * 1024:(ch + 1) * 1024])
            p.wT(p.sv, p.vc)

            lg_evac = 0
            for c in range(8):
                if c > 0:
                    p.wT(p.sv, lg_evac)
                for k in range(8):
                    ksl = slice(k * 1024 + c * 128, k * 1024 + (c + 1) * 128)
                    gsl = slice(k * 8, (k + 1) * 8)
                    p.MM(psC[:, 0:8], xn[:, ksl], gw_sb[:, gsl], start=(k == 0), stop=False)
                    p.MM(psC[:, 0:8], xnlo[:, ksl], gw_sb[:, gsl], start=False, stop=False)
                    gmm = p.MM(psC[:, 0:8], xn[:, ksl], gwlo_sb[:, gsl], start=False,
                               stop=(k == 7), inc=(k == 7))
                p.wV(p.spe, gmm)
                lg = g8[0]
                p.V("tensor_copy", lg[:, :], psC[:, 0:8])
                lg_evac = p.V("tensor_add", lg[:, :], lg[:, :], gbb_sb[:, :])
                p.V("reduce_max", gcl[0][:, :], lg[:, :], AX)
                p.V("tensor_single_scalar", g8[1][:, :], lg[:, :], gcl[0][:, :], ALU.is_ge)
                p.V("scalar_tensor_tensor", g8[2][:, :], g8[1][:, :], -1e30, lg[:, :],
                    ALU.mult, ALU.add)
                p.V("reduce_max", gcl[1][:, :], g8[2][:, :], AX)
                dv_ = p.V("tensor_single_scalar", g8[3][:, :], lg[:, :], gcl[0][:, :],
                          ALU.subtract)
                p.wA(p.sv, dv_)
                ev = p.A("activation", g8[3][:, :], g8[3][:, :], AF.Exp)
                m21 = p.V("tensor_single_scalar", gcl[2][:, :], gcl[1][:, :], gcl[0][:, :],
                          ALU.subtract)
                p.wA(p.sv, m21)
                e2v = p.A("activation", gcl[2][:, :], gcl[2][:, :], AF.Exp)
                p.wV(p.sa, e2v)
                p.V("tensor_scalar_add", gcl[2][:, :], gcl[2][:, :], 1.0)
                p.V("reciprocal", gcl[2][:, :], gcl[2][:, :])
                p.V("tensor_single_scalar", g8[4][:, :], lg[:, :], gcl[1][:, :], ALU.is_ge)
                p.wV(p.sa, ev)
                p.V("tensor_mul", g8[4][:, :], g8[4][:, :], g8[3][:, :])
                p.V("tensor_single_scalar", g8[4][:, :], g8[4][:, :], gcl[2][:, :], ALU.mult)
                p.V("tensor_scalar_mul", g8[4][:, :], g8[4][:, :], 0.5)
                cv = p.V("tensor_copy", gtb[:, :], g8[4][:, :])
                p.wT(p.sv, cv)
                trv = p.TR(psT[0:8, 0:128], gtb[:, :], ident[:, :])
                p.wV(p.spe, trv)
                p.V("tensor_copy", combT[0:8, c * 128:(c + 1) * 128], psT[0:8, 0:128])
            gate_v = p.vc

            for e in range(NE + 1):
                shared = (e == NE)
                for dc in range(32):
                    buf = w1b[dc % 2]
                    first = (half == 0 and e == 0 and dc < 2)
                    if not first:
                        p.wG(p.spe, mm1_grp[dc - 2] if dc >= 2 else mm1_grp[30 + dc])
                    src = ws1_p[dc] if shared else w1_p[e, dc]
                    dv = p.DQ(f"w{dc % 2}", buf[:, :], src.rearrange("r k c -> r (k c)"))
                    p.wT(p.sq(f"w{dc % 2}"), dv)
                    pr = psA[:, (dc % 2) * 1024:(dc % 2 + 1) * 1024]
                    if not first:
                        p.wT(p.sa, mm1_evac[dc - 2] if dc >= 2 else mm1_evac[30 + dc])
                    if dc == 0:
                        p.wT(p.sv, p.vc)
                    for k in range(8):
                        p.MM(pr[:, 0:512], buf[:, k * 128:(k + 1) * 128],
                             xn[:, k * 1024:k * 1024 + 512], start=(k == 0), stop=(k == 7))
                        p.MM(pr[:, 512:1024], buf[:, k * 128:(k + 1) * 128],
                             xn[:, k * 1024 + 512:(k + 1) * 1024], start=(k == 0),
                             stop=(k == 7), inc=(k == 7))
                    mm1_grp[dc] = p.pec
                    bias = bs1_sb[:, dc:dc + 1] if shared else \
                        b1_sb[:, e * 32 + dc:e * 32 + dc + 1]
                    p.wA(p.spe, mm1_grp[dc])
                    mm1_evac[dc] = p.A("activation", h[:, dc * 1024:(dc + 1) * 1024], pr,
                                       AF.Gelu, bias=bias)
                comb_v = 0
                if not shared:
                    p.wT(p.sv, gate_v)
                    p.wT(p.sv, mm2_yacc[6])
                    for c in range(8):
                        p.MM(psB[:, c * 128:(c + 1) * 128],
                             e8sel[:, e * 128:(e + 1) * 128],
                             combT[0:8, c * 128:(c + 1) * 128],
                             start=True, stop=True, inc=(c == 7))
                    p.wV(p.spe, p.pec)
                    comb_v = p.V("tensor_copy", cbc, psB[:, 0:1024])
                for dm in range(8):
                    buf = w2b[dm % 2]
                    first2 = (half == 0 and e == 0 and dm < 2)
                    if not first2:
                        p.wG(p.spe, mm2_grp[dm - 2] if dm >= 2 else mm2_grp[6 + dm])
                    src = ws2_p[dm] if shared else w2_p[e, dm]
                    dv = p.DQ(f"w{2 + dm % 2}", buf[:, :], src.rearrange("r k c -> r (k c)"))
                    p.wT(p.sq(f"w{2 + dm % 2}"), dv)
                    pr = psB[:, 0:1024] if dm % 2 == 0 else psA[:, 0:1024]
                    if dm % 2 == 0 and not shared:
                        p.wT(p.sv, comb_v)
                    if dm == 0 or dm % 2 == 1:
                        p.wT(p.sa, mm1_evac[31])
                    if not first2:
                        p.wT(p.sv, mm2_yacc[dm - 2] if dm >= 2 else mm2_yacc[6 + dm])
                    for k in range(32):
                        p.MM(pr[:, 0:512], buf[:, k * 128:(k + 1) * 128],
                             h[:, k * 1024:k * 1024 + 512], start=(k == 0), stop=(k == 31))
                        p.MM(pr[:, 512:1024], buf[:, k * 128:(k + 1) * 128],
                             h[:, k * 1024 + 512:(k + 1) * 1024], start=(k == 0),
                             stop=(k == 31), inc=(k == 31))
                    mm2_grp[dm] = p.pec
                    p.wA(p.spe, mm2_grp[dm])
                    p.wA(p.sv, prev_ztmp_read)
                    if shared:
                        zv = p.A("activation", ztmp, pr, AF.Identity,
                                 bias=bs2h_sb[:, dm:dm + 1], scale=halfc[:, :])
                        p.wV(p.sa, zv)
                        prev_ztmp_read = mm2_yacc[dm] = p.V(
                            "tensor_add", y[:, dm * 1024:(dm + 1) * 1024],
                            y[:, dm * 1024:(dm + 1) * 1024], ztmp)
                    else:
                        zv = p.A("activation", ztmp, pr, AF.Identity,
                                 bias=b2_sb[:, e * 8 + dm:e * 8 + dm + 1])
                        p.wV(p.sa, zv)
                        if e == 0:
                            prev_ztmp_read = mm2_yacc[dm] = p.V(
                                "tensor_mul", y[:, dm * 1024:(dm + 1) * 1024],
                                ztmp, cbc)
                        else:
                            p.V("tensor_mul", ztmp, ztmp, cbc)
                            prev_ztmp_read = mm2_yacc[dm] = p.V(
                                "tensor_add", y[:, dm * 1024:(dm + 1) * 1024],
                                y[:, dm * 1024:(dm + 1) * 1024], ztmp)
            uvh = [0] * 8
            for ch in range(8):
                t = xc2[ch % 2]
                o = xo2[ch % 2]
                if ch >= 2:
                    p.wS(p.sq(f"o{ch % 2}"), xout_dma[ch - 2])
                    p.wS(p.sv, uvh[ch - 2])
                dv = p.DQ(f"x{ch % 2}", t[:, :], x_src[ch * 128:(ch + 1) * 128, c0:c0 + 1024])
                p.wV(p.sq(f"x{ch % 2}"), dv)
                if ch >= 2:
                    p.wV(p.sq(f"o{ch % 2}"), xout_dma[ch - 2])
                uvh[ch] = uv = p.V("tensor_add", o, t[:, :], y[:, ch * 1024:(ch + 1) * 1024])
                p.wS(p.sv, uv)
                xout_dma[ch] = p.DQ(f"o{ch % 2}", x_dst[ch * 128:(ch + 1) * 128, c0:c0 + 1024], o)
            for par in range(2):
                p.wV(p.sq(f"o{par}"), p.streams[f"o{par}"][2])
        st.close()

    dbg = dt("dbg", [128, 52224], F32, kind="ExternalOutput").ap() if attn_stop else None

    def attention(x_src, x_dst):
        st = ExitStack()
        asb = lambda n, s, d: st.enter_context(nc.sbuf_tensor("at" + n, s, d))
        outu = asb("outu", [128, NG * T], BF16)
        q_fm = asb("qfm", [128, NG * T], BF16)
        k_fm = asb("kfm", [128, NG * T], BF16)
        v_fm = asb("vfm", [128, NG * T], BF16)
        tp1 = asb("tp1", [16, T], F32)
        u1 = asb("u1", [128, T], F32)
        tp3 = u1[0:16, :]
        rs1b = asb("rs1b", [16, T], BF16)
        sab = asb("sab", [16, T], BF16)
        scb = asb("scb", [16, T], BF16)
        gb_ = asb("gb_", [16, T], BF16)

        s1 = ExitStack()
        s1sb = lambda n, s, d: s1.enter_context(nc.sbuf_tensor("a1" + n, s, d))
        s1ps = lambda n, s, d: s1.enter_context(nc.psum_tensor("a1" + n, s, d))
        n16 = s1sb("n16", [16, T], F32)
        rn16 = s1sb("rn16", [16, T], F32)
        ckscr = s1sb("cks", [128, T], F32)
        wprod = s1sb("wpr", [128, T], BF16)
        sib = scb
        sob = gb_
        wvb_t = s1sb("wvb", [128, 8 * 128], BF16)
        wvb = [wvb_t, wvb_t]
        gath = s1sb("gath", [128, 16], F32)
        pref1 = s1sb("pf1", [128, 16], F32)
        pref2 = s1sb("pf2", [128, 16], F32)
        pub = s1sb("pub", [128, 16], F32)
        te_t = s1sb("te", [16, 1], F32)
        ga16 = s1sb("ga16", [16, 1], F32)
        epref = s1sb("epf", [16, 1], F32)
        xc2 = [s1sb(f"xc{i}", [128, 1024], F32) for i in range(2)]
        cbt = wprod[:, 0:1024]
        sqbt = wprod[:, 1024:2048]
        t1 = ckscr[:, 0:1024]
        mbc = u1[:, 0:1024]
        rstd = u1[:, 1024:2048]
        atA = s1ps("atA", [128, 2048], F32)
        atS = s1ps("atS", [16, 2048], F32)

        p.DS(n16[:, :], n16_p)
        p.DS(rn16[:, :], rn16_p)
        p.wV(p.sds, p.dsc)

        def chk(name, dumps):
            if attn_stop != name:
                return False
            p.wG(p.sv, p.vc)
            p.wG(p.sa, p.ac)
            p.wG(p.spe, p.pec)
            col = 0
            for ap, w in dumps:
                rows = ap.shape[0]
                nc.gpsimd.dma_start(out=dbg[0:rows, col:col + w], in_=ap,
                                    allow_slow_non_contiguous=True).then_inc(p.sdg, 16)
                p.dgc += 16
                col += w
            p.wG(p.sdg, p.dgc)
            return True

        for half in range(2):
            c0 = half * 1024
            if half == 1:
                p.wT(p.sv, p.vc)
                p.wS(p.sv, p.vc)
            layernorm(x_src, c0,
                      lambda ch: (None, outu[:, ch * T + c0:ch * T + c0 + 1024]),
                      xc2, cbt, sqbt, t1, mbc, rstd,
                      atA[:, 0:1024], atA[:, 1024:2048])
        p.wT(p.sv, p.vc)

        proj_grp = [0, 0]
        pair_evac = [0, 0]
        for pi, (wp, dest, func, bofs) in enumerate(
                ((wq_p, q_fm, AF.Sigmoid, 0), (wk_p, k_fm, AF.Sigmoid, 8),
                 (wv_p, v_fm, AF.Identity, 16))):
            for g in range(NG):
                idx = pi * NG + g
                buf = wvb[idx % 2]
                if idx >= 1:
                    p.wG(p.spe, max(proj_grp))
                dv = p.DQ("w0", buf[:, :], wp[g].rearrange("r k c -> r (k c)"))
                p.wT(p.sq("w0"), dv)
                for t in range(4):
                    pr = atA[:, (t % 2) * 512:(t % 2) * 512 + 512]
                    p.wT(p.sa, pair_evac[t % 2])
                    for k in range(8):
                        p.MM(pr, buf[:, k * 128:(k + 1) * 128],
                             outu[:, k * T + t * 512:k * T + (t + 1) * 512],
                             start=(k == 0), stop=(k == 7), inc=(k == 7))
                    proj_grp[idx % 2] = p.pec
                    p.wA(p.spe, p.pec)
                    pair_evac[t % 2] = p.A(
                        "activation", dest[:, g * T + t * 512:g * T + (t + 1) * 512],
                        pr, func, bias=bqkv_sb[:, bofs + g:bofs + g + 1])
        p.wV(p.sa, p.ac)
        if chk("proj", [(q_fm[:, :], NG * T), (k_fm[:, :], NG * T),
                        (v_fm[:, :], NG * T)]):
            s1.close(); st.close(); return

        for g in range(NG):
            p.V("tensor_reduce", pub[:, g:g + 1], k_fm[:, g * T:(g + 1) * T], AX, ALU.add)
            p.V("tensor_reduce", pub[:, 8 + g:8 + g + 1], q_fm[:, g * T:(g + 1) * T], AX,
                ALU.add)

        def exchange128(pub_ap, cci, cco, pref_t):
            p.wG(p.sv, p.vc)
            dv = p.DG(cci.ap(), pub_ap)
            p.wG(p.sdg, dv)
            nc.gpsimd.collective_compute(
                "AllGather", ALU.bypass, replica_groups=[list(range(NC))],
                ins=[cci.ap().opt()], outs=[cco.ap().opt()]).then_inc(p.scc, 1)
            p.ccc += 1
            p.wG(p.scc, p.ccc)
            p.V("memset", pref_t[:, :], 0.0)
            for r in range(NC):
                p.wG(p.sv, p.vc)
                dv = p.DG(gath[:, :], cco.ap()[r * 128:(r + 1) * 128, :])
                p.wV(p.sdg, dv)
                p.V("scalar_tensor_tensor", pref_t[:, :], gath[:, :],
                    rmask[:, r:r + 1], pref_t[:, :], ALU.mult, ALU.add)
            return p.vc

        exchange128(pub[:, :], cc_in[0], cc_out[0], pref1)

        def scan_w_sel(data_fm, other_fm, pref_col, g, first, last):
            p.V("tensor_tensor_scan", ckscr[:, :], data_fm,
                data_fm[:, 0:1].broadcast_to((128, T)), pref_col, ALU.add, ALU.bypass)
            wv = p.V("scalar_tensor_tensor", wprod[:, :], other_fm, EPS_AT,
                     ckscr[:, :], ALU.add, ALU.mult)
            p.wT(p.sv, wv)
            for t in range(4):
                p.MM(atS[0:16, t * 512:(t + 1) * 512], selc[:, g * 16:(g + 1) * 16],
                     wprod[:, t * 512:(t + 1) * 512], start=(first and t < 4), stop=False)
                p.MM(atS[0:16, t * 512:(t + 1) * 512], seleps[:, g * 16:(g + 1) * 16],
                     other_fm[:, t * 512:(t + 1) * 512], start=False,
                     stop=last, inc=(t == 3))
            return p.pec

        for g in range(NG):
            if g > 0:
                p.wV(p.spe, p.pec)
            scan_w_sel(k_fm[:, g * T:(g + 1) * T], q_fm[:, g * T:(g + 1) * T],
                       pref1[:, g:g + 1], g, g == 0, g == 7)
        p.wV(p.spe, p.pec)
        p.V("tensor_scalar_add", tp3[:, :], atS[0:16, :], 64.0 * EPS_AT * EPS_AT)
        p.V("reciprocal", tp1[:, :], tp3[:, :])
        p.V("tensor_copy", rs1b[:, :], tp1[:, :])
        sib_v = p.V("tensor_mul", sib[:, :], n16[:, :], tp1[:, :])
        p.wT(p.sv, p.vc)
        for g in range(NG):
            if g > 0:
                p.wV(p.spe, p.pec)
            scan_w_sel(q_fm[:, g * T:(g + 1) * T], k_fm[:, g * T:(g + 1) * T],
                       pref1[:, 8 + g:8 + g + 1], g, g == 0, g == 7)
        p.wV(p.spe, p.pec)
        p.V("tensor_scalar_add", tp3[:, :], atS[0:16, :], 64.0 * EPS_AT * EPS_AT)
        p.V("reciprocal", tp1[:, :], tp3[:, :])
        sob_v = p.V("tensor_mul", sob[:, :], n16[:, :], tp1[:, :])
        if chk("s12", [(sib[0:16, :], T), (sob[0:16, :], T), (pref1[:, :], 16),
                       (pub[:, :], 16)]):
            s1.close(); st.close(); return

        def bc16(src_b16, g, ps):
            for t in range(4):
                p.MM(ps[:, t * 512:(t + 1) * 512], seltc[:, g * 128:(g + 1) * 128],
                     src_b16[:, t * 512:(t + 1) * 512], start=True, stop=True,
                     inc=(t == 3))
            return p.pec

        p.wT(p.sv, sob_v)
        for g in range(NG):
            mmv = bc16(sob, g, atA)
            p.wV(p.spe, mmv)
            kso_v = p.V("scalar_tensor_tensor", wprod[:, :], k_fm[:, g * T:(g + 1) * T],
                        1.0, atA[:, :], ALU.mult, ALU.mult, accum_out=pub[:, g:g + 1])
            p.wT(p.sv, kso_v)
            mmv = bc16(sib, g, atA)
            p.wV(p.spe, mmv)
            qsi_v = p.V("scalar_tensor_tensor", wprod[:, :], q_fm[:, g * T:(g + 1) * T],
                        1.0, atA[:, :], ALU.mult, ALU.mult, accum_out=pub[:, 8 + g:8 + g + 1])
            p.wT(p.sv, qsi_v)

        exchange128(pub[:, :], cc_in[1], cc_out[1], pref2)

        def c34_pass(bcast_b, base_fm_getter, dot_fm_getter, pref_ofs):
            for g in range(NG):
                if g > 0:
                    p.wV(p.spe, p.pec)
                mmv = bc16(bcast_b, g, atA)
                p.wV(p.spe, mmv)
                p.V("scalar_tensor_tensor", wprod[:, :], base_fm_getter(g), 1.0,
                    atA[:, :], ALU.mult, ALU.mult)
                p.V("tensor_tensor_scan", ckscr[:, :], wprod[:, :],
                    wprod[:, 0:1].broadcast_to((128, T)),
                    pref2[:, pref_ofs + g:pref_ofs + g + 1], ALU.add, ALU.bypass)
                wv = p.V("scalar_tensor_tensor", wprod[:, :], dot_fm_getter(g), EPS_AT,
                         ckscr[:, :], ALU.add, ALU.mult)
                p.wT(p.sv, wv)
                dot_fm = dot_fm_getter(g)
                for t in range(4):
                    p.MM(atS[0:16, t * 512:(t + 1) * 512], selc[:, g * 16:(g + 1) * 16],
                         wprod[:, t * 512:(t + 1) * 512], start=(g == 0), stop=False)
                    p.MM(atS[0:16, t * 512:(t + 1) * 512], seleps[:, g * 16:(g + 1) * 16],
                         dot_fm[:, t * 512:(t + 1) * 512], start=False,
                         stop=(g == 7), inc=(t == 3))
            return p.pec

        kg = lambda g: k_fm[:, g * T:(g + 1) * T]
        qg = lambda g: q_fm[:, g * T:(g + 1) * T]
        c34_pass(sob, kg, qg, 0)
        p.wV(p.spe, p.pec)
        p.V("tensor_scalar_add", tp3[:, :], atS[0:16, :], 64.0 * EPS_AT * EPS_AT)
        csk = p.V("tensor_mul", tp1[:, :], tp3[:, :], rn16[:, :])
        p.wA(p.sv, csk)
        sa_v = p.A("activation", sab[:, :], tp1[:, :], AF.Sigmoid)
        p.wT(p.sv, csk)
        c34_pass(sib, qg, kg, 8)
        p.wV(p.spe, p.pec)
        p.wV(p.sa, sa_v)
        p.V("tensor_scalar_add", tp3[:, :], atS[0:16, :], 64.0 * EPS_AT * EPS_AT)
        css = p.V("tensor_mul", tp1[:, :], tp3[:, :], rn16[:, :])
        p.wA(p.sv, css)
        e_v = p.A("activation", tp1[:, :], tp1[:, :], AF.Exp)
        p.wV(p.sa, e_v)
        if chk("c34", [(sab[0:16, :], T), (tp1[0:16, :], T), (pref2[:, :], 16)]):
            s1.close(); st.close(); return

        tev = p.V("tensor_reduce", te_t[:, :], tp1[:, :], AX, ALU.add)
        p.wG(p.sv, tev)
        dv = p.DG(cc3_in.ap(), te_t[:, :])
        p.wG(p.sdg, dv)
        nc.gpsimd.collective_compute(
            "AllGather", ALU.bypass, replica_groups=[list(range(NC))],
            ins=[cc3_in.ap().opt()], outs=[cc3_out.ap().opt()]).then_inc(p.scc, 1)
        p.ccc += 1
        p.wG(p.scc, p.ccc)
        p.V("memset", epref[:, :], 0.0)
        for r in range(NC):
            p.wG(p.sv, p.vc)
            dv = p.DG(ga16[:, :], cc3_out.ap()[r * 16:(r + 1) * 16, :])
            p.wV(p.sdg, dv)
            p.V("scalar_tensor_tensor", epref[:, :], ga16[:, :], rmask[0:16, r:r + 1],
                epref[:, :], ALU.mult, ALU.add)

        p.V("tensor_tensor_scan", tp3[:, :], tp1[:, :],
            tp1[:, 0:1].broadcast_to((16, T)), epref[:, :], ALU.add, ALU.bypass)
        p.V("reciprocal", tp3[:, :], tp3[:, :])
        p.V("tensor_mul", tp3[:, :], tp1[:, :], tp3[:, :])
        p.V("tensor_mul", scb[:, :], tp3[:, :], n16[:, :])
        gv = p.V("tensor_mul", gb_[:, :], sab[:, :], rs1b[:, :])
        p.wT(p.sv, gv)
        if chk("sc", [(scb[0:16, :], T), (gb_[0:16, :], T), (epref[0:16, :], 1)]):
            s1.close(); st.close(); return
        s1.close()

        s2 = ExitStack()
        s2sb = lambda n, s, d: s2.enter_context(nc.sbuf_tensor("a2" + n, s, d))
        s2ps = lambda n, s, d: s2.enter_context(nc.psum_tensor("a2" + n, s, d))
        gbc = s2sb("gbc", [128, T], BF16)
        pub4 = s2sb("pb4", [128, NG * 64], F32)
        sgth = s2sb("sgt", [128, NG * 64], F32)
        spref_b = s2sb("spb", [128, NG * 64], BF16)
        ktm_all = s2sb("ktmA", [128, 16 * 128], BF16)
        vtm_all = s2sb("vtmA", [128, 16 * 128], BF16)
        amu_all = s2sb("amuA", [128, 16 * 256], BF16)
        snap = s2sb("snp", [128, 128], BF16)
        s_acc = s2sb("sacc", [128, 128], F32)
        wosb = [s2sb(f"wo{i}", [128, 8 * 128], BF16) for i in range(2)]
        zt5 = s2sb("zt5", [128, 512], F32)
        e1t = [zt5[:, 0:256], zt5[:, 256:512]]
        xc5 = [s2sb(f"xc5{i}", [128, 512], F32) for i in range(2)]
        xo5 = [s2sb(f"xo5{i}", [128, 512], F32) for i in range(2)]
        pBig = s2ps("pB", [128, 2048], F32)
        pS = s2ps("pS", [128, 128], F32)
        pT = s2ps("pT", [128, 2048], BF16)
        pW = s2ps("pW", [128, 512], F32)

        tr_evac = [0, 0]
        amu_evac = [0, 0]
        out_evac = [0, 0]
        supd = 0
        j_ng, j_nc, j_steps = NG, NCH, 5
        if attn_stop and attn_stop.startswith("j:"):
            parts = attn_stop.split(":")
            j_ng, j_nc = int(parts[1]), int(parts[2])
            if len(parts) > 3:
                j_steps = int(parts[3])
        for g in range(j_ng):
            gofs = g * T
            p.wT(p.sv, p.vc)
            mmv = bc16(scb, g, pBig)
            p.wV(p.spe, mmv)
            vp = p.V("scalar_tensor_tensor", v_fm[:, gofs:gofs + T],
                     v_fm[:, gofs:gofs + T], 1.0, pBig[:, :], ALU.mult, ALU.mult)
            p.wT(p.sv, vp)
            mmv = bc16(gb_, g, pBig)
            p.wV(p.spe, mmv)
            gvv = p.V("tensor_copy", gbc[:, :], pBig[:, :])
            p.V("memset", s_acc[:, :], 0.0)
            p.V("memset", snap[:, :], 0.0)
            p.wT(p.sv, p.vc)
            for c in range(j_nc):
                if j_nc == 0:
                    break
                pa = c % 2
                cs = slice(gofs + c * 128, gofs + (c + 1) * 128)
                tslot = pT[:, pa * 1024:pa * 1024 + 256]
                p.wT(p.sv, tr_evac[pa])
                p.TR(tslot[:, 0:128], k_fm[:, cs], ident[:, :])
                trv = p.TR(tslot[:, 128:256], v_fm[:, cs], ident[:, :])
                p.wV(p.spe, trv)
                p.V("tensor_copy", ktm_all[:, c * 128:(c + 1) * 128], tslot[:, 0:128])
                tr_evac[pa] = p.V("tensor_copy", vtm_all[:, c * 128:(c + 1) * 128],
                                  tslot[:, 128:256])
                if j_steps < 2:
                    continue
                pgA = pBig[:, pa * 1024:pa * 1024 + 128]
                pgB = pBig[:, pa * 1024 + 512:pa * 1024 + 640]
                p.wT(p.sv, amu_evac[pa])
                p.MM(pgA, k_fm[0:64, cs], q_fm[0:64, cs], start=True, stop=True,
                     tp=(0, 0))
                gmv = p.MM(pgB, k_fm[64:128, cs], q_fm[64:128, cs],
                           start=True, stop=True, tp=(64, 0), inc=True)
                p.wV(p.spe, gmv)
                p.V("tensor_mul", amu_all[:, c * 256:c * 256 + 128], pgA, umask[:, :])
                amu_evac[pa] = p.V("tensor_mul", amu_all[:, c * 256 + 128:c * 256 + 256],
                                   pgB, umask[:, :])
            if j_steps < 3:
                continue
            p.wT(p.sv, p.vc)
            sdone = 0
            for c in range(j_nc):
                cs = slice(gofs + c * 128, gofs + (c + 1) * 128)
                poi = pBig[:, 1024:1152]
                pon = pBig[:, 1536:1664]
                if c > 0:
                    p.wT(p.sv, sdone)
                supd = p.MM(pS[:, :], ktm_all[:, c * 128:(c + 1) * 128],
                            vtm_all[:, c * 128:(c + 1) * 128], start=True,
                            stop=True, inc=True)
                p.wT(p.sv, out_evac[0])
                p.MM(poi[0:64, :], vtm_all[:, c * 128:c * 128 + 64],
                     amu_all[:, c * 256:c * 256 + 128],
                     start=True, stop=True, tp=(0, 0))
                omv = p.MM(poi[64:128, :], vtm_all[:, c * 128 + 64:(c + 1) * 128],
                           amu_all[:, c * 256 + 128:c * 256 + 256],
                           start=True, stop=True, tp=(0, 64), inc=True)
                if c > 0:
                    p.MM(pon[0:64, :], snap[0:64, 0:64], q_fm[0:64, cs],
                         start=True, stop=True, tp=(0, 0))
                    omv = p.MM(pon[64:128, :], snap[64:128, 64:128], q_fm[64:128, cs],
                               start=True, stop=True, tp=(64, 64), inc=True)
                p.wV(p.spe, supd)
                p.V("tensor_add", s_acc[:, :], s_acc[:, :], pS[:, :])
                sdone = p.V("tensor_copy", snap[:, :], s_acc[:, :])
                p.wV(p.spe, omv)
                if c > 0:
                    p.V("tensor_mul", e1t[0][:, 0:128], poi,
                        gbc[:, c * 128:(c + 1) * 128])
                    p.V("tensor_mul", e1t[0][:, 128:256], pon,
                        gbc[:, c * 128:(c + 1) * 128])
                    out_evac[0] = p.V("tensor_add", outu[:, cs], e1t[0][:, 0:128],
                                      e1t[0][:, 128:256])
                else:
                    out_evac[0] = p.V("tensor_mul", outu[:, cs], poi,
                                      gbc[:, 0:128])
            p.V("tensor_copy", pub4[0:64, g * 64:(g + 1) * 64], s_acc[0:64, 0:64])
            sv_ = p.V("tensor_copy", pub4[64:128, g * 64:(g + 1) * 64],
                      s_acc[64:128, 64:128])
            p.wT(p.sv, sv_)

        if chk("j", [(outu[:, :], NG * T), (pub4[:, :], NG * 64)]) or \
           (attn_stop and attn_stop.startswith("j:") and
            chk(attn_stop, [(outu[:, :], NG * T), (pub4[:, :], NG * 64)])):
            s2.close(); st.close(); return
        p.wG(p.sv, p.vc)
        dv = p.DG(cc4_in.ap(), pub4[:, :])
        p.wG(p.sdg, dv)
        nc.gpsimd.collective_compute(
            "AllGather", ALU.bypass, replica_groups=[list(range(NC))],
            ins=[cc4_in.ap().opt()], outs=[cc4_out.ap().opt()]).then_inc(p.scc, 1)
        p.ccc += 1
        p.wG(p.scc, p.ccc)
        p.wV(p.scc, p.ccc)
        p.V("memset", pub4[:, :], 0.0)
        for r in range(NC):
            p.wG(p.sv, p.vc)
            dv = p.DG(sgth[:, :], cc4_out.ap()[r * 128:(r + 1) * 128, :])
            p.wV(p.sdg, dv)
            p.V("scalar_tensor_tensor", pub4[:, :], sgth[:, :], rmask[:, r:r + 1],
                pub4[:, :], ALU.mult, ALU.add)
        spv = p.V("tensor_copy", spref_b[:, :], pub4[:, :])

        for g in range(NG):
            gofs = g * T
            p.wT(p.sv, p.vc)
            mmv = bc16(gb_, g, pBig)
            p.wV(p.spe, mmv)
            gvv = p.V("tensor_copy", gbc[:, :], pBig[:, :])
            p.wT(p.sv, gvv)
            p.wT(p.sv, spv)
            for t in range(4):
                pr = pW[:, :] if t % 2 == 0 else pBig[:, 0:512]
                ts = slice(gofs + t * 512, gofs + (t + 1) * 512)
                p.MM(pr[0:64, :], spref_b[0:64, g * 64:(g + 1) * 64],
                     q_fm[0:64, ts], start=True, stop=True, tp=(0, 0))
                lmv = p.MM(pr[64:128, :], spref_b[64:128, g * 64:(g + 1) * 64],
                           q_fm[64:128, ts], start=True, stop=True, tp=(64, 64),
                           inc=True)
                p.wV(p.spe, lmv)
                p.V("tensor_mul", zt5[:, :], pr, gbc[:, t * 512:(t + 1) * 512])
                ov = p.V("tensor_add", outu[:, ts], outu[:, ts], zt5[:, :])
                p.wT(p.sv, ov)

        if chk("l", [(outu[:, :], NG * T)]):
            s2.close(); st.close(); return
        if attn_stop == "xres":
            p.wG(p.sv, p.vc)
            for ch in range(8):
                p.DG(dbg[:, ch * 512:(ch + 1) * 512], x_src[ch * 128:(ch + 1) * 128, 0:512])
            p.wG(p.sdg, p.dgc)
            s2.close(); st.close(); return
        wo_grp = [0, 0]
        xo_dma = [0] * 32
        uv_hist = [0] * 32
        prev_z_read = p.vc
        p.wA(p.sv, p.vc)
        p.wT(p.sv, p.vc)
        for dm in range(8):
            buf = wosb[dm % 2]
            if dm >= 2:
                p.wG(p.spe, wo_grp[dm % 2])
            dv = p.DQ(f"w{dm % 2}", buf[:, :], wo_p[dm].rearrange("r k c -> r (k c)"))
            p.wT(p.sq(f"w{dm % 2}"), dv)
            for t in range(4):
                n = dm * 4 + t
                pr = pW[:, :] if t % 2 == 0 else pBig[:, 0:512]
                if n >= 2:
                    p.wT(p.sa, z_evac[n - 2])
                for g in range(NG):
                    p.MM(pr, buf[:, g * 128:(g + 1) * 128],
                         outu[:, g * T + t * 512:g * T + (t + 1) * 512],
                         start=(g == 0), stop=(g == 7), inc=(g == 7))
                wo_grp[dm % 2] = p.pec
                p.wA(p.spe, p.pec)
                p.wA(p.sv, prev_z_read)
                if n == 0:
                    z_evac = [0] * 32
                z_evac[n] = p.A("activation", zt5[:, :], pr, AF.Identity,
                                bias=bo_sb[:, dm:dm + 1])
                if attn_stop == "wo" and dm == 0 and t in (0, 1):
                    p.wG(p.sa, z_evac[n])
                    nc.gpsimd.dma_start(out=dbg[:, t * 512:(t + 1) * 512],
                                        in_=zt5[:, :]).then_inc(p.sdg, 16)
                    p.dgc += 16
                xt_ = xc5[n % 2]
                xo_ = xo5[n % 2]
                if n >= 2:
                    p.wG(p.sq(f"o{n % 2}"), xo_dma[n - 2])
                    p.wG(p.sv, uv_hist[n - 2])
                dvx = p.DQ(f"xg{n % 2}", xt_[:, :], x_src[dm * 128:(dm + 1) * 128,
                                                          t * 512:(t + 1) * 512])
                p.wV(p.sq(f"xg{n % 2}"), dvx)
                p.wV(p.sa, z_evac[n])
                if n >= 2:
                    p.wV(p.sq(f"o{n % 2}"), xo_dma[n - 2])
                prev_z_read = uv = p.V("tensor_add", xo_[:, :], xt_[:, :], zt5[:, :])
                uv_hist[n] = uv
                p.wS(p.sv, uv)
                xo_dma[n] = p.DQ(f"o{n % 2}",
                                 x_dst[dm * 128:(dm + 1) * 128, t * 512:(t + 1) * 512],
                                 xo_[:, :])
                if attn_stop == "n0" and n == 0:
                    p.wG(p.sv, uv)
                    p.DG(dbg[:, 0:512], xt_[:, :])
                    p.DG(dbg[:, 512:1024], zt5[:, :])
                    p.DG(dbg[:, 1024:1536], xo_[:, :])
                    p.wG(p.sdg, p.dgc)
        for par in range(2):
            p.wV(p.sq(f"o{par}"), p.streams[f"o{par}"][2])
        if attn_stop == "xres2":
            p.wG(p.sds, p.dsc)
            for ch in range(8):
                p.DG(dbg[:, ch * 512:(ch + 1) * 512], x_dst[ch * 128:(ch + 1) * 128, 0:512])
            p.wG(p.sdg, p.dgc)
        s2.close()
        st.close()

    load_consts()
    p.drain_barrier()
    if "ffn1" in phases:
        ffn_call(x_in, xres, "f1")
        p.drain_barrier()
    else:
        for ch in range(8):
            p.DS(xres[ch * 128:(ch + 1) * 128, :], x_in[ch * 128:(ch + 1) * 128, :])
        p.drain_barrier()
    if "attn" in phases:
        attention(xres, xres)
        p.drain_barrier()
    if "ffn2" in phases:
        ffn_call(xres, x_out, "f2")
    else:
        for ch in range(8):
            p.DS(x_out[ch * 128:(ch + 1) * 128, :], xres[ch * 128:(ch + 1) * 128, :])
    p.wS(p.sds, p.dsc)
    p.wG(p.sdg, p.dgc)
    for nm, (sem_, kind, cntv) in p.streams.items():
        if cntv:
            (p.wG if kind == "g" else p.wS)(sem_, cntv)
    p.ctx.close()
    outer.close()
    return nc



def _shared_inputs(inputs):
    f32c = lambda a: np.ascontiguousarray(np.asarray(a, dtype=np.float32))
    wq = f32c(inputs["L0_attn_Wq"]); wk = f32c(inputs["L0_attn_Wk"])
    wv = f32c(inputs["L0_attn_Wv"]); wo = f32c(inputs["L0_attn_Wo"])
    w1 = f32c(inputs["L0_ffn_w1"]); w2 = f32c(inputs["L0_ffn_w2"])
    ws1 = f32c(inputs["L0_ffn_ws1"]); ws2 = f32c(inputs["L0_ffn_ws2"])

    def blk(w, kc, dcc):
        return np.ascontiguousarray(
            w.reshape(kc, 128, dcc, 128).transpose(2, 1, 0, 3))

    d = {}
    d["wq_p"] = blk(wq, 8, 8)
    d["wk_p"] = blk(wk, 8, 8)
    d["wv_p"] = blk(wv, 8, 8)
    d["wo_p"] = blk(wo, 8, 8)
    d["w1_p"] = np.ascontiguousarray(
        w1.reshape(NE, 8, 128, 32, 128).transpose(0, 3, 2, 1, 4))
    d["w2_p"] = np.ascontiguousarray(
        w2.reshape(NE, 32, 128, 8, 128).transpose(0, 3, 2, 1, 4))
    d["ws1_p"] = blk(ws1, 8, 32)
    d["ws2_p"] = blk(ws2, 32, 8)
    gw = f32c(inputs["L0_ffn_gate_w"])
    d["gw_p"] = np.ascontiguousarray(
        gw.reshape(8, 128, 8).transpose(1, 0, 2).reshape(128, 64))
    d["gb_p"] = f32c(inputs["L0_ffn_gate_b"]).reshape(1, 8)
    d["gbb_p"] = np.ascontiguousarray(
        np.broadcast_to(f32c(inputs["L0_ffn_gate_b"]).reshape(1, 8), (128, 8)))
    d["b1_p"] = np.ascontiguousarray(
        f32c(inputs["L0_ffn_b1"]).reshape(NE, 32, 128).transpose(0, 2, 1))
    d["b2_p"] = np.ascontiguousarray(
        f32c(inputs["L0_ffn_b2"]).reshape(NE, 8, 128).transpose(0, 2, 1))
    d["bs1_p"] = np.ascontiguousarray(f32c(inputs["L0_ffn_bs1"]).reshape(32, 128).T)
    d["bs2_p"] = np.ascontiguousarray(f32c(inputs["L0_ffn_bs2"]).reshape(8, 128).T)
    d["bq_p"] = np.ascontiguousarray(f32c(inputs["L0_attn_bq"]).reshape(8, 128).T)
    d["bk_p"] = np.ascontiguousarray(f32c(inputs["L0_attn_bk"]).reshape(8, 128).T)
    d["bv_p"] = np.ascontiguousarray(f32c(inputs["L0_attn_bv"]).reshape(8, 128).T)
    d["bo_p"] = np.ascontiguousarray(f32c(inputs["L0_attn_bo"]).reshape(8, 128).T)
    d["umask_p"] = np.triu(np.ones((128, 128), np.float32))
    d["ident_p"] = np.eye(128, dtype=np.float32)
    d["ones_p"] = np.ones((128, 128), np.float32)
    sel = np.zeros((128, 128), np.float32)
    for g in range(NG):
        sel[0:64, g * 16 + 2 * g] = 1.0
        sel[64:128, g * 16 + 2 * g + 1] = 1.0
    d["sel_p"] = sel
    d["seleps_p"] = sel * 1e-6
    selt = np.zeros((16, NG * 128), np.float32)
    for g in range(NG):
        selt[2 * g, g * 128:g * 128 + 64] = 1.0
        selt[2 * g + 1, g * 128 + 64:g * 128 + 128] = 1.0
    d["selt_p"] = selt
    e8 = np.zeros((8, NE * 128), np.float32)
    for e in range(NE):
        e8[e, e * 128:(e + 1) * 128] = 1.0
    d["e8_p"] = e8
    return d


def make_in_maps(inputs, nc_build=None):
    shared = _shared_inputs(inputs)
    x = np.asarray(inputs["L0_attn_x"], dtype=np.float32).reshape(NC * T, D)
    in_maps = []
    for i in range(NC):
        m = dict(shared)
        m["x_in"] = np.ascontiguousarray(x[i * T:(i + 1) * T, :].T)
        pos = (np.arange(T, dtype=np.float64) + 1.0 + i * T)
        m["n16_p"] = np.ascontiguousarray(
            np.broadcast_to(pos.astype(np.float32), (16, T)))
        m["rn16_p"] = np.ascontiguousarray(
            np.broadcast_to((1.0 / pos).astype(np.float32), (16, T)))
        rm = np.zeros((128, 8), np.float32)
        rm[:, :i] = 1.0
        m["rmask_p"] = rm
        in_maps.append(m)
    return in_maps


def kernel(**inputs):
    nc = build_program()
    in_maps = make_in_maps(inputs)
    res = run_bass_kernel_spmd(nc, in_maps, core_ids=list(range(NC)))
    out = np.concatenate([np.asarray(r["x_out"]).T for r in res.results], axis=0)
    return out.reshape(1, NC * T, D)
